# version 06
# baseline (speedup 1.0000x reference)
import numpy as np
import ml_dtypes
from contextlib import ExitStack

import concourse.bass as bass
import concourse.mybir as mybir
from concourse.bass_utils import run_bass_kernel_spmd

F32 = mybir.dt.float32
BF16 = mybir.dt.bfloat16
I32 = mybir.dt.int32
AF = mybir.ActivationFunctionType
ALU = mybir.AluOpType
BFNP = ml_dtypes.bfloat16

NCORE = 8
DM = 1024
DSS = 2048
NST = 16
EOUT = 2080
LTOT = 16384
TC = LTOT // NCORE
DSL = DSS // NCORE
EPS = 1e-5
L3_DEBUG_FFN_ONLY = False
L3_DEBUG_NO_YP = False
L3_DEBUG_YP_ONLY = False


ENGINES = ("sync", "vector", "scalar", "tensor", "gpsimd")


class Step:
    __slots__ = ("eng", "fn", "deps", "kind", "post", "idx", "small", "dpost")

    def __init__(self, eng, fn, deps, kind, small=False):
        self.eng = eng
        self.fn = fn
        self.deps = deps
        self.kind = kind
        self.post = None
        self.idx = None
        self.small = small


class Sched:
    def __init__(self, nc, ctx):
        self.nc = nc
        self.steps = {e: [] for e in ENGINES}
        self.sems = {e: ctx.enter_context(nc.semaphore(f"sem_{e}")) for e in ENGINES}
        self.dsems = {e: ctx.enter_context(nc.semaphore(f"dsem_{e}")) for e in ENGINES}

    def step(self, eng, fn, deps=(), kind="compute", small=False):
        s = Step(eng, fn, list(deps), kind, small=small)
        s.idx = len(self.steps[eng])
        self.steps[eng].append(s)
        return s

    def dma(self, eng, fn, deps=()):
        return self.step(eng, fn, deps, kind="dma")

    def emit(self, block):
        for e in ENGINES:
            cumd = 0
            cumc = 0
            for s in self.steps[e]:
                cumc += 1
                s.post = cumc
                if s.kind == "dma":
                    cumd += 16 * getattr(s.fn, "n_dma")
                    s.dpost = cumd

        sched = self

        def body_for(eng_name):
            def body(eng):
                waited = {e: 0 for e in ENGINES}
                for s in sched.steps[eng_name]:
                    for d in s.deps:
                        if d.eng == eng_name:
                            continue
                        if waited[d.eng] < d.post:
                            eng.wait_ge(sched.sems[d.eng], d.post)
                            waited[d.eng] = d.post
                    instrs = s.fn(eng)
                    if s.kind == "dma":
                        for ins in instrs:
                            ins.then_inc(sched.dsems[eng_name], 16)
                        eng.wait_ge(sched.dsems[eng_name], s.dpost)
                        eng.sem_inc(sched.sems[eng_name], 1)
                    elif s.small:
                        eng.drain().then_inc(sched.sems[eng_name], 1)
                    else:
                        instrs[-1].then_inc(sched.sems[eng_name], 1)
            return body

        block.sync(body_for("sync"))
        block.vector(body_for("vector"))
        block.scalar(body_for("scalar"))
        block.tensor(body_for("tensor"))
        block.gpsimd(body_for("gpsimd"))


def dmas(n):
    def wrap(fn):
        fn.n_dma = n
        return fn
    return wrap


def emit_ln(sch, p, xg, ng, T, dep_x, dep_sums=()):
    nc = sch.nc

    def sq_fn(eng):
        out = []
        for g in range(ng):
            out.append(eng.activation(out=p["sq"][g][:, 0:T], in_=xg[g][:, 0:T],
                                      func=AF.Square))
        return out
    s_sq = sch.step("scalar", sq_fn, deps=list(dep_x))

    def sums_fn(eng):
        out = []
        for g in range(ng):
            out.append(eng.matmul(p["ps_s1"][0:1, 0:T], p["ones1"][:],
                                  xg[g][:, 0:T], start=(g == 0), stop=(g == ng - 1)))
        for g in range(ng):
            out.append(eng.matmul(p["ps_s2"][0:1, 0:T], p["ones1"][:],
                                  p["sq"][g][:, 0:T], start=(g == 0), stop=(g == ng - 1)))
        return out
    s_sums = sch.step("tensor", sums_fn, deps=list(dep_x) + [s_sq] + list(dep_sums))

    def rows_fn(eng):
        o1 = eng.tensor_scalar(out=p["rowa"][:, 0:T], in0=p["ps_s1"][0:1, 0:T],
                               scalar1=1.0 / DM, scalar2=None, op0=ALU.mult)
        o2 = eng.tensor_scalar(out=p["rowb"][:, 0:T], in0=p["ps_s2"][0:1, 0:T],
                               scalar1=1.0 / DM, scalar2=None, op0=ALU.mult)
        o3 = eng.tensor_tensor(out=p["rowc"][:, 0:T], in0=p["rowa"][:, 0:T],
                               in1=p["rowa"][:, 0:T], op=ALU.mult)
        o4 = eng.tensor_tensor(out=p["rowb"][:, 0:T], in0=p["rowb"][:, 0:T],
                               in1=p["rowc"][:, 0:T], op=ALU.subtract)
        return [o1, o2, o3, o4]
    s_rows = sch.step("vector", rows_fn, deps=[s_sums])

    def rstd_fn(eng):
        o1 = eng.activation(out=p["rowc"][:, 0:T], in_=p["rowb"][:, 0:T],
                            func=AF.Ln, bias=p["cb_eps"])
        o2 = eng.activation(out=p["rowb"][:, 0:T], in_=p["rowc"][:, 0:T],
                            func=AF.Exp, scale=p["cb_mhalf"])
        return [o1, o2]
    s_rstd = sch.step("scalar", rstd_fn, deps=[s_rows] + list(p.get("cb_dep", [])))

    def mursd_fn(eng):
        return [eng.tensor_tensor(out=p["rowc"][:, 0:T], in0=p["rowa"][:, 0:T],
                                  in1=p["rowb"][:, 0:T], op=ALU.mult)]
    s_mursd = sch.step("vector", mursd_fn, deps=[s_rstd])

    def bc_fn(eng):
        o1 = eng.matmul(p["ps_r"][:, 0:T], p["onesr"][:], p["rowb"][:, 0:T],
                        start=True, stop=True)
        o2 = eng.matmul(p["ps_m"][:, 0:T], p["onesr"][:], p["rowc"][:, 0:T],
                        start=True, stop=True)
        return [o1, o2]
    s_bc = sch.step("tensor", bc_fn, deps=[s_mursd, s_rstd])
    return s_bc, s_sq


def build_L1():
    T = 512
    NCH = TC // T
    nc = bass.Bass("TRN2", target_bir_lowering=False)
    X = nc.dram_tensor("X", [DM, TC + 3], F32, kind="ExternalInput").ap()
    HM = nc.dram_tensor("HM", [1, 3], F32, kind="ExternalInput").ap()
    LNG = nc.dram_tensor("LNG", [128, 8], F32, kind="ExternalInput").ap()
    LNB = nc.dram_tensor("LNB", [128, 8], F32, kind="ExternalInput").ap()
    WU1T = nc.dram_tensor("WU1T", [DM, DSS], F32, kind="ExternalInput").ap()
    WLLT = nc.dram_tensor("WLLT", [DSS, EOUT], F32, kind="ExternalInput").ap()
    CW = nc.dram_tensor("CW", [128, 64], F32, kind="ExternalInput").ap()
    CB = nc.dram_tensor("CB", [128, 16], F32, kind="ExternalInput").ap()
    BLL = nc.dram_tensor("BLL", [128, 17], F32, kind="ExternalInput").ap()

    O_DELTA = nc.dram_tensor("O_DELTA", [DSS, TC], F32, kind="ExternalOutput").ap()
    O_XMS = nc.dram_tensor("O_XMS", [DSS, TC], BF16, kind="ExternalOutput").ap()
    O_BC = nc.dram_tensor("O_BC", [32, TC], BF16, kind="ExternalOutput").ap()
    O_XN = nc.dram_tensor("O_XN", [DM, TC], BF16, kind="ExternalOutput").ap()
    O_HS = nc.dram_tensor("O_HS", [128, 48], F32, kind="ExternalOutput").ap()
    O_XNHB = nc.dram_tensor("O_XNHB", [128, 24], BF16, kind="ExternalOutput").ap()

    with ExitStack() as ctx:
        e = ctx.enter_context
        wu1 = [e(nc.sbuf_tensor(f"wu1_{k}", [128, DSS], BF16)) for k in range(8)]
        wll = [e(nc.sbuf_tensor(f"wll_{k}", [128, EOUT], BF16)) for k in range(16)]
        wtmp = [e(nc.sbuf_tensor(f"wtmp{i}", [128, EOUT], F32)) for i in range(2)]
        lng = e(nc.sbuf_tensor("lng", [128, 8], F32))
        lnb = e(nc.sbuf_tensor("lnb", [128, 8], F32))
        cw = e(nc.sbuf_tensor("cw", [128, 64], F32))
        cb = e(nc.sbuf_tensor("cb", [128, 16], F32))
        bll = e(nc.sbuf_tensor("bll", [128, 17], F32))
        hm = e(nc.sbuf_tensor("hm", [128, 3], F32))
        ones1 = e(nc.sbuf_tensor("ones1", [128, 1], F32))
        onesr = e(nc.sbuf_tensor("onesr", [1, 128], F32))
        cbank = e(nc.sbuf_tensor("cbank", [128, 4], F32))
        xc = [e(nc.sbuf_tensor(f"xc{g}", [128, T], F32)) for g in range(8)]
        xh = e(nc.sbuf_tensor("xh", [128, 24], F32))
        xnhb = e(nc.sbuf_tensor("xnhb", [128, 24], BF16))
        xnbf = [e(nc.sbuf_tensor(f"xnbf{g}", [128, T], BF16)) for g in range(8)]
        tmpA = e(nc.sbuf_tensor("tmpA", [128, T], F32))
        tmpB = e(nc.sbuf_tensor("tmpB", [128, T], F32))
        sq = [e(nc.sbuf_tensor(f"sq{g}", [128, T], F32)) for g in range(8)]
        rowa = e(nc.sbuf_tensor("rowa", [1, T], F32))
        rowb = e(nc.sbuf_tensor("rowb", [1, T], F32))
        rowc = e(nc.sbuf_tensor("rowc", [1, T], F32))
        convb = e(nc.sbuf_tensor("convb", [128, T + 3], F32))
        hsave = e(nc.sbuf_tensor("hsave", [128, 48], F32))
        ct = [e(nc.sbuf_tensor(f"ct{i}", [128, T], F32)) for i in range(4)]
        cs = [e(nc.sbuf_tensor(f"cs{i}", [128, T], F32)) for i in range(2)]
        xms = [e(nc.sbuf_tensor(f"xms{g}", [128, T], BF16)) for g in range(16)]
        zz = e(nc.sbuf_tensor("zz", [128, T], F32))
        az = e(nc.sbuf_tensor("az", [128, T], F32))
        lz = e(nc.sbuf_tensor("lz", [128, T], F32))
        mz = e(nc.sbuf_tensor("mz", [128, T], F32))
        dl = [e(nc.sbuf_tensor(f"dl{i}", [128, T], F32)) for i in range(2)]
        bcb = e(nc.sbuf_tensor("bcb", [32, T], BF16))

        ps_s1 = e(nc.psum_tensor("ps_s1", [1, T], F32))
        ps_s2 = e(nc.psum_tensor("ps_s2", [1, T], F32))
        ps_r = e(nc.psum_tensor("ps_r", [128, T], F32))
        ps_m = e(nc.psum_tensor("ps_m", [128, T], F32))
        ps_xm = [e(nc.psum_tensor(f"ps_xm{i}", [128, T], F32)) for i in range(2)]
        ps_pp = [e(nc.psum_tensor(f"ps_pp{i}", [128, T], F32)) for i in range(2)]

        sch = Sched(nc, ctx)
        block = e(nc.Block())
        def ones_fn(eng):
            o1 = eng.memset(ones1[:], 1.0)
            o2 = eng.memset(onesr[:], 1.0)
            o3 = eng.memset(cbank[:, 0:1], float(EPS))
            o4 = eng.memset(cbank[:, 1:2], -0.5)
            o5 = eng.memset(cbank[:, 2:3], 1.0)
            return [o1, o2, o3, o4, o5]
        s_ones = sch.step("vector", ones_fn, small=True)
        P = dict(sq=sq, rowa=rowa, rowb=rowb, rowc=rowc, ones1=ones1,
                 onesr=onesr, ps_s1=ps_s1, ps_s2=ps_s2, ps_r=ps_r, ps_m=ps_m,
                 cb_eps=cbank[0:1, 0:1], cb_mhalf=cbank[0:1, 1:2],
                 cb_dep=[s_ones])

        @dmas(7)
        def consts_fn(eng):
            return [
                eng.dma_start(out=lng[:], in_=LNG),
                eng.dma_start(out=lnb[:], in_=LNB),
                eng.dma_start(out=cw[:], in_=CW),
                eng.dma_start(out=cb[:], in_=CB),
                eng.dma_start(out=bll[:], in_=BLL),
                eng.dma_start(out=hm[:], in_=HM.broadcast_to([128, 3])),
                eng.dma_start(out=xh.ap().rearrange("p (g t) -> p g t", g=8),
                              in_=X.rearrange("(g p) t -> p g t", p=128)[:, :, 0:3]),
            ]
        s_consts = sch.dma("sync", consts_fn)

        wsteps = []
        cvt_steps = []
        for k in range(24):
            dst = wu1[k] if k < 8 else wll[k - 8]
            src = (WU1T[128 * k:128 * (k + 1), :] if k < 8
                   else WLLT[128 * (k - 8):128 * (k - 7), :])
            w = DSS if k < 8 else EOUT
            deps = [cvt_steps[k - 2]] if k >= 2 else []

            @dmas(1)
            def wl_fn(eng, src=src, k=k, w=w):
                return [eng.dma_start(out=wtmp[k % 2][:, 0:w], in_=src)]
            sw = sch.dma("sync", wl_fn, deps=deps)
            wsteps.append(sw)

            def cv_fn(eng, dst=dst, k=k, w=w):
                return [eng.tensor_copy(out=dst[:, 0:w], in_=wtmp[k % 2][:, 0:w])]
            sc = sch.step("vector", cv_fn, deps=[sw])
            cvt_steps.append(sc)

        def hsq_fn(eng):
            return [eng.activation(out=tmpA[:, 0:24], in_=xh[:], func=AF.Square)]
        s_hsq = sch.step("scalar", hsq_fn, deps=[s_consts])

        def hsums_fn(eng):
            outs = []
            for g in range(8):
                outs.append(eng.matmul(ps_s1[0:1, 0:3], ones1[:],
                                       xh[:, 3 * g:3 * g + 3],
                                       start=(g == 0), stop=(g == 7)))
            for g in range(8):
                outs.append(eng.matmul(ps_s2[0:1, 0:3], ones1[:],
                                       tmpA[:, 3 * g:3 * g + 3],
                                       start=(g == 0), stop=(g == 7)))
            return outs
        s_hsums = sch.step("tensor", hsums_fn, deps=[s_hsq, s_consts, s_ones],
                           small=True)

        def hrows_fn(eng):
            outs = []
            outs.append(eng.tensor_scalar(out=rowa[:, 0:3], in0=ps_s1[0:1, 0:3],
                                          scalar1=1.0 / DM, scalar2=None, op0=ALU.mult))
            outs.append(eng.tensor_scalar(out=rowb[:, 0:3], in0=ps_s2[0:1, 0:3],
                                          scalar1=1.0 / DM, scalar2=None, op0=ALU.mult))
            eng.drain()
            outs.append(eng.tensor_tensor(out=rowc[:, 0:3], in0=rowa[:, 0:3],
                                          in1=rowa[:, 0:3], op=ALU.mult))
            eng.drain()
            outs.append(eng.tensor_tensor(out=rowb[:, 0:3], in0=rowb[:, 0:3],
                                          in1=rowc[:, 0:3], op=ALU.subtract))
            return outs
        s_hrows = sch.step("vector", hrows_fn, deps=[s_hsums], small=True)

        def hrstd_fn(eng):
            o1 = eng.activation(out=rowc[:, 0:3], in_=rowb[:, 0:3], func=AF.Ln,
                                bias=cbank[0:1, 0:1])
            eng.drain()
            o2 = eng.activation(out=rowb[:, 0:3], in_=rowc[:, 0:3], func=AF.Exp,
                                scale=cbank[0:1, 1:2])
            return [o1, o2]
        s_hrstd = sch.step("scalar", hrstd_fn, deps=[s_hrows, s_ones], small=True)

        def hmursd_fn(eng):
            return [eng.tensor_tensor(out=rowc[:, 0:3], in0=rowa[:, 0:3],
                                      in1=rowb[:, 0:3], op=ALU.mult)]
        s_hmursd = sch.step("vector", hmursd_fn, deps=[s_hrstd], small=True)

        def hbc_fn(eng):
            o1 = eng.matmul(ps_r[:, 0:3], onesr[:], rowb[:, 0:3], start=True, stop=True)
            o2 = eng.matmul(ps_m[:, 0:3], onesr[:], rowc[:, 0:3], start=True, stop=True)
            return [o1, o2]
        s_hbc = sch.step("tensor", hbc_fn, deps=[s_hmursd])

        def hxn_fn(eng):
            outs = []
            for g in range(8):
                outs.append(eng.tensor_tensor(out=tmpA[:, 0:3], in0=xh[:, 3 * g:3 * g + 3],
                                              in1=ps_r[:, 0:3], op=ALU.mult))
                eng.drain()
                outs.append(eng.tensor_tensor(out=xh[:, 3 * g:3 * g + 3], in0=tmpA[:, 0:3],
                                              in1=ps_m[:, 0:3], op=ALU.subtract))
                eng.drain()
            return outs
        s_hxn = sch.step("vector", hxn_fn, deps=[s_hbc, s_hsums], small=True)

        def hsq2_fn(eng):
            return [eng.activation(out=tmpA[:, 0:24], in_=xh[:], func=AF.Square)]
        s_hsq2 = sch.step("scalar", hsq2_fn, deps=[s_hxn])

        def hsums2_fn(eng):
            outs = []
            for g in range(8):
                outs.append(eng.matmul(ps_s1[0:1, 0:3], ones1[:],
                                       xh[:, 3 * g:3 * g + 3],
                                       start=(g == 0), stop=(g == 7)))
            for g in range(8):
                outs.append(eng.matmul(ps_s2[0:1, 0:3], ones1[:],
                                       tmpA[:, 3 * g:3 * g + 3],
                                       start=(g == 0), stop=(g == 7)))
            return outs
        s_hsums2 = sch.step("tensor", hsums2_fn, deps=[s_hsq2, s_hxn, s_hrows],
                            small=True)

        s_hrows2 = sch.step("vector", hrows_fn, deps=[s_hsums2], small=True)
        s_hrstd2 = sch.step("scalar", hrstd_fn, deps=[s_hrows2], small=True)
        s_hmursd2 = sch.step("vector", hmursd_fn, deps=[s_hrstd2], small=True)
        s_hbc2 = sch.step("tensor", hbc_fn, deps=[s_hmursd2, s_hxn], small=True)

        def hXn_fn(eng):
            outs = []
            for g in range(8):
                outs.append(eng.tensor_tensor(out=tmpA[:, 0:3], in0=xh[:, 3 * g:3 * g + 3],
                                              in1=ps_r[:, 0:3], op=ALU.mult))
                eng.drain()
                outs.append(eng.tensor_tensor(out=tmpB[:, 0:3], in0=tmpA[:, 0:3],
                                              in1=ps_m[:, 0:3], op=ALU.subtract))
                eng.drain()
                outs.append(eng.tensor_scalar(out=tmpA[:, 0:3], in0=tmpB[:, 0:3],
                                              scalar1=lng[:, g:g + 1], scalar2=lnb[:, g:g + 1],
                                              op0=ALU.mult, op1=ALU.add))
                eng.drain()
                outs.append(eng.tensor_tensor(out=xnhb[:, 3 * g:3 * g + 3], in0=tmpA[:, 0:3],
                                              in1=hm[:], op=ALU.mult))
                eng.drain()
            return outs
        s_hXn = sch.step("vector", hXn_fn, deps=[s_hbc2], small=True)

        hx_steps = []
        for g in range(16):
            def hxm_fn(eng, g=g):
                outs = []
                for k in range(8):
                    outs.append(eng.matmul(ps_xm[g % 2][:, 0:3],
                                           wu1[k][:, 128 * g:128 * (g + 1)],
                                           xnhb[:, 3 * k:3 * k + 3],
                                           start=(k == 0), stop=(k == 7)))
                return outs
            deps = [s_hXn] + cvt_steps[:8] + ([hx_steps[-2][1]] if g >= 2 else [])
            s_mm = sch.step("tensor", hxm_fn, deps=deps, small=True)

            def hev_fn(eng, g=g):
                return [eng.copy(out=hsave[:, 3 * g:3 * g + 3], in_=ps_xm[g % 2][:, 0:3])]
            s_ev = sch.step("scalar", hev_fn, deps=[s_mm], small=True)
            hx_steps.append((s_mm, s_ev))

        prev_chunk_consumers = []
        prev_xn_out = None
        prev_conv = {}
        prev_silu = {}
        prev_xst = {}
        prev_pp_chain = {}
        prev_pp_consumers = []
        store_steps = []
        for c in range(NCH):
            @dmas(8)
            def xload_fn(eng, c=c):
                return [eng.dma_start(out=xc[g][:],
                                      in_=X[128 * g:128 * (g + 1), 3 + c * T:3 + (c + 1) * T])
                        for g in range(8)]
            s_xload = sch.dma("sync", xload_fn, deps=prev_chunk_consumers)

            s_bc1, s_sq1 = emit_ln(sch, P, xc, 8, T, [s_xload],
                                   dep_sums=([s_hrows2] if c == 0 else []))

            def xn_fn(eng):
                outs = []
                for g in range(8):
                    outs.append(eng.tensor_tensor(out=tmpA[:], in0=xc[g][:],
                                                  in1=ps_r[:], op=ALU.mult))
                    outs.append(eng.tensor_tensor(out=xc[g][:], in0=tmpA[:],
                                                  in1=ps_m[:], op=ALU.subtract))
                return outs
            s_xn = sch.step("vector", xn_fn, deps=[s_bc1, s_sq1])

            s_bc2, s_sq2 = emit_ln(sch, P, xc, 8, T, [s_xn])

            def Xn_fn(eng):
                outs = []
                for g in range(8):
                    outs.append(eng.tensor_tensor(out=tmpA[:], in0=xc[g][:],
                                                  in1=ps_r[:], op=ALU.mult))
                    outs.append(eng.tensor_tensor(out=tmpB[:], in0=tmpA[:],
                                                  in1=ps_m[:], op=ALU.subtract))
                    outs.append(eng.tensor_scalar(out=xnbf[g][:], in0=tmpB[:],
                                                  scalar1=lng[:, g:g + 1],
                                                  scalar2=lnb[:, g:g + 1],
                                                  op0=ALU.mult, op1=ALU.add))
                return outs
            dep_xn_war = [prev_xn_out] if prev_xn_out is not None else []
            s_Xn = sch.step("vector", Xn_fn, deps=[s_bc2, s_sq2] + dep_xn_war)
            prev_chunk_consumers = [s_Xn, s_sq2]

            @dmas(8)
            def xnstore_fn(eng, c=c):
                return [eng.dma_start(out=O_XN[128 * g:128 * (g + 1), c * T:(c + 1) * T],
                                      in_=xnbf[g][:]) for g in range(8)]
            s_xnstore = sch.dma("gpsimd", xnstore_fn, deps=[s_Xn])
            prev_xn_out = s_xnstore
            store_steps.append(s_xnstore)

            xm_mm_prev = None
            for g in range(16):
                def xm_fn(eng, g=g):
                    outs = []
                    for k in range(8):
                        outs.append(eng.matmul(ps_xm[g % 2][:],
                                               wu1[k][:, 128 * g:128 * (g + 1)],
                                               xnbf[k][:], start=(k == 0), stop=(k == 7)))
                    return outs
                deps = [s_Xn]
                if c == 0 and g < 2:
                    deps.append(hx_steps[14 + g][1])
                if (c, g - 2) in prev_conv or (c - 1, g + 14) in prev_conv:
                    pass
                key = (c, g - 2) if g >= 2 else (c - 1, 14 + g)
                if key in prev_silu:
                    deps.append(prev_silu[key][0])
                s_mm = sch.step("tensor", xm_fn, deps=deps)

                def evac_fn(eng, g=g):
                    return [eng.copy(out=convb[:, 3:T + 3], in_=ps_xm[g % 2][:])]
                dep_e = [s_mm]
                pk = (c, g - 1) if g >= 1 else (c - 1, 15)
                if pk in prev_conv:
                    dep_e.append(prev_conv[pk])
                if c == 0 and g == 0:
                    dep_e.append(hx_steps[15][1])
                s_evac = sch.step("scalar", evac_fn, deps=dep_e, small=True)

                def conv_fn(eng, g=g):
                    outs = []
                    outs.append(eng.tensor_copy(out=convb[:, 0:3],
                                                in_=hsave[:, 3 * g:3 * g + 3]))
                    outs.append(eng.tensor_copy(out=hsave[:, 3 * g:3 * g + 3],
                                                in_=convb[:, T:T + 3]))
                    eng.drain()
                    for k in range(4):
                        outs.append(eng.tensor_scalar(out=ct[k][:], in0=convb[:, k:k + T],
                                                      scalar1=cw[:, 4 * g + k:4 * g + k + 1],
                                                      scalar2=None, op0=ALU.mult))
                    outs.append(eng.tensor_tensor(out=cs[0][:], in0=ct[0][:], in1=ct[1][:],
                                                  op=ALU.add))
                    outs.append(eng.tensor_tensor(out=cs[1][:], in0=ct[2][:], in1=ct[3][:],
                                                  op=ALU.add))
                    outs.append(eng.tensor_tensor(out=cs[0][:], in0=cs[0][:], in1=cs[1][:],
                                                  op=ALU.add))
                    return outs
                dep_c = [s_evac]
                pk2 = (c, g - 1) if g >= 1 else (c - 1, 15)
                if pk2 in prev_silu:
                    dep_c.append(prev_silu[pk2][1])
                s_conv = sch.step("vector", conv_fn, deps=dep_c)
                prev_conv[(c, g)] = s_conv

                def silu_fn(eng, g=g):
                    return [eng.activation(out=xms[g][:], in_=cs[0][:], func=AF.Silu,
                                           bias=cb[:, g:g + 1])]
                dep_s = [s_conv]
                if prev_pp_consumers:
                    dep_s.append(prev_pp_consumers[-1])
                if (c - 1, g) in prev_xst:
                    dep_s.append(prev_xst[(c - 1, g)])
                s_silu = sch.step("scalar", silu_fn, deps=dep_s)
                prev_silu[(c, g)] = (s_evac, s_silu)

                @dmas(1)
                def xms_store_fn(eng, c=c, g=g):
                    return [eng.dma_start(
                        out=O_XMS[128 * g:128 * (g + 1), c * T:(c + 1) * T],
                        in_=xms[g][:])]
                s_xst = sch.dma("gpsimd", xms_store_fn, deps=[s_silu])
                prev_xst[(c, g)] = s_xst
                store_steps.append(s_xst)

            silu_all = [prev_silu[(c, g)][1] for g in range(16)]

            pp_last_mm = None
            pp_dl_steps = []
            for t in range(17):
                def pp_fn(eng, t=t):
                    outs = []
                    w = 128 if t < 16 else 32
                    for k in range(16):
                        outs.append(eng.matmul(
                            ps_pp[t % 2][0:w, :],
                            wll[k][:, 128 * t:128 * t + w],
                            xms[k][:], start=(k == 0), stop=(k == 15)))
                    return outs
                deps = silu_all + cvt_steps[8:]
                if t >= 2:
                    deps.append(pp_dl_steps[t - 2][0])
                elif (c - 1, 15 + t) in prev_pp_chain:
                    deps.append(prev_pp_chain[(c - 1, 15 + t)])
                s_ppmm = sch.step("tensor", pp_fn, deps=deps)

                if t < 16:
                    def sp1_fn(eng, t=t):
                        o1 = eng.tensor_scalar(out=zz[:], in0=ps_pp[t % 2][:],
                                               scalar1=bll[:, t:t + 1], scalar2=None,
                                               op0=ALU.add)
                        o2 = eng.tensor_scalar(out=mz[:], in0=zz[:], scalar1=0.0,
                                               scalar2=None, op0=ALU.max)
                        return [o1, o2]
                    s_sp1 = sch.step("vector", sp1_fn, deps=[s_ppmm])

                    def sp2_fn(eng, t=t):
                        o0 = eng.activation(out=az[:], in_=zz[:], func=AF.Abs)
                        o1 = eng.activation(out=az[:], in_=az[:], func=AF.Exp, scale=-1.0)
                        o2 = eng.activation(out=lz[:], in_=az[:], func=AF.Ln,
                                            bias=cbank[:, 2:3])
                        return [o0, o1, o2]
                    s_sp2 = sch.step("scalar", sp2_fn, deps=[s_sp1])

                    def sp3_fn(eng, t=t):
                        return [eng.tensor_tensor(out=dl[t % 2][:], in0=mz[:], in1=lz[:],
                                                  op=ALU.add)]
                    dep3 = [s_sp2]
                    if t >= 2:
                        dep3.append(pp_dl_steps[t - 2][1])
                    s_sp3 = sch.step("vector", sp3_fn, deps=dep3)

                    @dmas(1)
                    def dst_fn(eng, c=c, t=t):
                        return [eng.dma_start(
                            out=O_DELTA[128 * t:128 * (t + 1), c * T:(c + 1) * T],
                            in_=dl[t % 2][:])]
                    s_dst = sch.dma("gpsimd", dst_fn, deps=[s_sp3])
                    pp_dl_steps.append((s_sp1, s_dst))
                    if t == 15:
                        prev_pp_chain[(c, 16)] = s_sp1
                    store_steps.append(s_dst)
                else:
                    def bc_fn2(eng, t=t):
                        return [eng.tensor_scalar(out=bcb[:], in0=ps_pp[t % 2][0:32, :],
                                                  scalar1=bll[0:32, 16:17], scalar2=None,
                                                  op0=ALU.add)]
                    s_bcr = sch.step("vector", bc_fn2, deps=[s_ppmm], small=True)

                    @dmas(1)
                    def bcst_fn(eng, c=c):
                        return [eng.dma_start(out=O_BC[:, c * T:(c + 1) * T], in_=bcb[:])]
                    s_bcst = sch.dma("gpsimd", bcst_fn, deps=[s_bcr])
                    store_steps.append(s_bcst)
                    prev_pp_chain[(c, 15)] = s_bcr
                    prev_pp_consumers = [s_ppmm]
                pp_last_mm = s_ppmm

        @dmas(2)
        def dbg_fn(eng):
            return [eng.dma_start(out=O_HS, in_=hsave[:]),
                    eng.dma_start(out=O_XNHB, in_=xnhb[:])]
        s_dbg = sch.dma("gpsimd", dbg_fn, deps=[hx_steps[15][1], s_hXn])
        store_steps.append(s_dbg)

        def fin_fn(eng):
            return [eng.memset(rowa[0:1, 0:1], 0.0)]
        sch.step("vector", fin_fn, deps=store_steps)

        sch.emit(block)
    return nc


def build_L2():
    T = 512
    NCH = LTOT // T
    NG = DSL // 128
    nc = bass.Bass("TRN2", target_bir_lowering=False)
    DELTA = nc.dram_tensor("DELTA", [DSL, LTOT], F32, kind="ExternalInput").ap()
    XMS = nc.dram_tensor("XMS", [DSL, LTOT], BF16, kind="ExternalInput").ap()
    BFI = nc.dram_tensor("BFI", [NST, LTOT], BF16, kind="ExternalInput").ap()
    CFI = nc.dram_tensor("CFI", [NST, LTOT], BF16, kind="ExternalInput").ap()
    XNF = nc.dram_tensor("XNF", [DM, LTOT], BF16, kind="ExternalInput").ap()
    ALOG = nc.dram_tensor("ALOG", [DSL, NST], F32, kind="ExternalInput").ap()
    WU2T = nc.dram_tensor("WU2T", [DM, DSL], F32, kind="ExternalInput").ap()
    WDT = nc.dram_tensor("WDT", [DSL, DM], F32, kind="ExternalInput").ap()
    O_YP = nc.dram_tensor("O_YP", [DM, LTOT], F32, kind="ExternalOutput").ap()

    NT = NST * T

    with ExitStack() as ctx:
        e = ctx.enter_context
        wu2 = [e(nc.sbuf_tensor(f"wu2_{k}", [128, DSL], BF16)) for k in range(8)]
        wdt = [e(nc.sbuf_tensor(f"wdt_{g}", [128, DM], BF16)) for g in range(NG)]
        wtmp = [e(nc.sbuf_tensor(f"wtmp{i}", [128, DM // 2], F32)) for i in range(2)]
        alog = [e(nc.sbuf_tensor(f"alog{g}", [128, NST], F32)) for g in range(NG)]
        Aa = [e(nc.sbuf_tensor(f"Aa{g}", [128, NST], F32)) for g in range(NG)]
        iA = [e(nc.sbuf_tensor(f"iA{g}", [128, NST], F32)) for g in range(NG)]
        eAn = e(nc.sbuf_tensor("eAn", [128, NST], F32))
        hst = [e(nc.sbuf_tensor(f"hst{g}", [128, NST], F32)) for g in range(NG)]
        delta = [[e(nc.sbuf_tensor(f"delta{p}_{g}", [128, T], F32)) for g in range(NG)]
                 for p in range(2)]
        xmsb = [[e(nc.sbuf_tensor(f"xms{p}_{g}", [128, T], BF16)) for g in range(NG)]
                for p in range(2)]
        xnc = [e(nc.sbuf_tensor(f"xnc_{k}", [128, T], BF16)) for k in range(8)]
        bbc = e(nc.sbuf_tensor("bbc", [128, NT], BF16))
        cbc = e(nc.sbuf_tensor("cbc", [128, NT], BF16))
        abar = e(nc.sbuf_tensor("abar", [128, NT], F32))
        am1 = e(nc.sbuf_tensor("am1", [128, NT], BF16))
        uu = e(nc.sbuf_tensor("uu", [128, NT], BF16))
        hh = [e(nc.sbuf_tensor(f"hh{g}", [128, NT], BF16)) for g in range(NG)]
        qq = e(nc.sbuf_tensor("qq", [128, NT], BF16))
        t1 = e(nc.sbuf_tensor("t1", [128, NT // 2], BF16))
        t2 = e(nc.sbuf_tensor("t2", [128, NT // 4], BF16))
        ybf = [e(nc.sbuf_tensor(f"ybf{g}", [128, T], BF16)) for g in range(NG)]
        xg = [e(nc.sbuf_tensor(f"xg{g}", [128, T], BF16)) for g in range(NG)]
        Yb = [[e(nc.sbuf_tensor(f"Yb{p}_{g}", [128, T], BF16)) for g in range(NG)]
              for p in range(2)]

        evac = [e(nc.sbuf_tensor(f"evac{m}", [128, T], F32)) for m in range(2)]
        ps_xg = [e(nc.psum_tensor(f"ps_xg{g}", [128, T], F32)) for g in range(NG)]
        ps_yp = [e(nc.psum_tensor(f"ps_yp{i}", [128, T], F32)) for i in range(2)]

        sch = Sched(nc, ctx)
        block = e(nc.Block())

        @dmas(2)
        def alog_fn(eng):
            return [eng.dma_start(out=alog[g][:], in_=ALOG[128 * g:128 * (g + 1), :])
                    for g in range(NG)]
        s_alog = sch.dma("sync", alog_fn)

        aprep_steps = []
        for g in range(NG):
            def aexp_fn(eng, g=g):
                o1 = eng.activation(out=Aa[g][:], in_=alog[g][:], func=AF.Exp)
                o2 = eng.activation(out=eAn[:], in_=alog[g][:], func=AF.Exp, scale=-1.0)
                return [o1, o2]
            s_ae = sch.step("scalar", aexp_fn,
                            deps=[s_alog] + ([aprep_steps[-1]] if aprep_steps else []),
                            small=True)

            def aneg_fn(eng, g=g):
                o1 = eng.tensor_scalar(out=Aa[g][:], in0=Aa[g][:], scalar1=-1.0,
                                       scalar2=None, op0=ALU.mult)
                o2 = eng.tensor_scalar(out=iA[g][:], in0=eAn[:], scalar1=-1.0,
                                       scalar2=None, op0=ALU.mult)
                o3 = eng.memset(hst[g][:], 0.0)
                return [o1, o2, o3]
            s_an = sch.step("vector", aneg_fn, deps=[s_ae], small=True)
            aprep_steps.append(s_an)

        cvt = []
        i = 0
        for k in range(10):
            dst = wu2[k] if k < 8 else wdt[k - 8]
            w = DSL if k < 8 else DM
            SRC = WU2T if k < 8 else WDT
            row = 128 * k if k < 8 else 128 * (k - 8)
            for h in range(2):
                w2 = w // 2

                @dmas(1)
                def wl_fn(eng, SRC=SRC, row=row, h=h, w2=w2, i=i):
                    return [eng.dma_start(out=wtmp[i % 2][:, 0:w2],
                                          in_=SRC[row:row + 128, h * w2:(h + 1) * w2])]
                sw = sch.dma("sync", wl_fn, deps=[cvt[i - 2]] if i >= 2 else [])

                def cv_fn(eng, dst=dst, i=i, h=h, w2=w2):
                    return [eng.tensor_copy(out=dst[:, h * w2:(h + 1) * w2],
                                            in_=wtmp[i % 2][:, 0:w2])]
                sc = sch.step("vector", cv_fn, deps=[sw])
                cvt.append(sc)
                i += 1

        prev = {}
        store_steps = []
        for c in range(NCH):
            p = c % 2

            @dmas(2 * NG)
            def load_fn(eng, c=c, p=p):
                outs = []
                for g in range(NG):
                    outs.append(eng.dma_start(
                        out=delta[p][g][:],
                        in_=DELTA[128 * g:128 * (g + 1), c * T:(c + 1) * T]))
                    outs.append(eng.dma_start(
                        out=xmsb[p][g][:],
                        in_=XMS[128 * g:128 * (g + 1), c * T:(c + 1) * T]))
                return outs
            dep_l = []
            if c >= 2:
                dep_l = [prev[(c - 2, 'scan', NG - 1)]]
            s_load = sch.dma("sync", load_fn, deps=dep_l)

            @dmas(8)
            def xnload_fn(eng, c=c):
                return [eng.dma_start(
                    out=xnc[k][:],
                    in_=XNF[128 * k:128 * (k + 1), c * T:(c + 1) * T])
                    for k in range(8)]
            dep_xl = [prev[(c - 1, 'xg', NG - 1)]] if c >= 1 else []
            s_xnload = sch.dma("sync", xnload_fn, deps=dep_xl)

            @dmas(2)
            def bc_fn(eng, c=c):
                o1 = eng.dma_start(
                    out=bbc.ap().rearrange("p (n t) -> p n t", n=NST),
                    in_=BFI[:, c * T:(c + 1) * T].unsqueeze(0).broadcast_to([128, NST, T]))
                o2 = eng.dma_start(
                    out=cbc.ap().rearrange("p (n t) -> p n t", n=NST),
                    in_=CFI[:, c * T:(c + 1) * T].unsqueeze(0).broadcast_to([128, NST, T]))
                return [o1, o2]
            dep_b = [prev[(c - 1, 'scan', NG - 1)]] if c >= 1 else []
            s_bcast = sch.dma("sync", bc_fn, deps=dep_b)

            for g in range(NG):
                NH2 = NST // 2

                def xqq_fn(eng, c=c, g=g, p=p):
                    return [eng.tensor_tensor(
                        out=qq.ap().rearrange("p (n t) -> p n t", n=NST),
                        in0=xmsb[p][g].ap().unsqueeze(1).broadcast_to([128, NST, T]),
                        in1=bbc.ap().rearrange("p (n t) -> p n t", n=NST),
                        op=ALU.mult)]
                dep_x0 = [s_load, s_bcast]
                if (c, 'ytail', g - 1) in prev:
                    dep_x0.append(prev[(c, 'ytail', g - 1)])
                elif (c - 1, 'ytail', NG - 1) in prev:
                    dep_x0.append(prev[(c - 1, 'ytail', NG - 1)])
                s_xqq = sch.step("vector", xqq_fn, deps=dep_x0)

                half_steps = []
                for hf in range(2):
                    n0 = hf * NH2

                    def abar_fn(eng, c=c, g=g, p=p, n0=n0):
                        outs = []
                        for n in range(n0, n0 + NH2):
                            outs.append(eng.activation(
                                out=abar[:, n * T:(n + 1) * T], in_=delta[p][g][:],
                                func=AF.Exp, scale=Aa[g][:, n:n + 1]))
                        return outs
                    dep_a = [s_load, aprep_steps[g]]
                    key = (c, 'scan' + 'AB'[hf], g - 1)
                    key2 = (c - 1, 'scan' + 'AB'[hf], NG - 1)
                    if key in prev:
                        dep_a.append(prev[key])
                    elif key2 in prev:
                        dep_a.append(prev[key2])
                    s_abar = sch.step("scalar", abar_fn, deps=dep_a)
                    prev[(c, 'abar' + 'AB'[hf], g)] = s_abar

                    def scanh_fn(eng, c=c, g=g, p=p, n0=n0):
                        outs = []
                        for n in range(n0, n0 + NH2):
                            outs.append(eng.tensor_scalar(
                                out=am1[:, n * T:(n + 1) * T],
                                in0=abar[:, n * T:(n + 1) * T],
                                scalar1=iA[g][:, n:n + 1], scalar2=iA[g][:, n:n + 1],
                                op0=ALU.mult, op1=ALU.subtract))
                        lo, hi = n0 * T, (n0 + NH2) * T
                        outs.append(eng.tensor_tensor(out=uu[:, lo:hi],
                                                      in0=am1[:, lo:hi],
                                                      in1=qq[:, lo:hi], op=ALU.mult))
                        for n in range(n0, n0 + NH2):
                            init = (hst[g][:, n:n + 1] if c == 0
                                    else hh[g][:, (n + 1) * T - 1:(n + 1) * T])
                            outs.append(eng.tensor_tensor_scan(
                                out=hh[g][:, n * T:(n + 1) * T],
                                data0=abar[:, n * T:(n + 1) * T],
                                data1=uu[:, n * T:(n + 1) * T],
                                initial=init,
                                op0=ALU.mult, op1=ALU.add))
                        return outs
                    s_scanh = sch.step("vector", scanh_fn, deps=[s_abar, s_xqq])
                    prev[(c, 'scan' + 'AB'[hf], g)] = s_scanh
                    half_steps.append(s_scanh)

                def ytail_fn(eng, c=c, g=g):
                    outs = []
                    outs.append(eng.tensor_tensor(out=qq[:], in0=hh[g][:], in1=cbc[:],
                                                  op=ALU.mult))
                    outs.append(eng.tensor_tensor(out=t1[:], in0=qq[:, 0:NT // 2],
                                                  in1=qq[:, NT // 2:], op=ALU.add))
                    outs.append(eng.tensor_tensor(out=t2[:], in0=t1[:, 0:NT // 4],
                                                  in1=t1[:, NT // 4:], op=ALU.add))
                    outs.append(eng.tensor_tensor(out=t1[:, 0:NT // 8], in0=t2[:, 0:NT // 8],
                                                  in1=t2[:, NT // 8:], op=ALU.add))
                    outs.append(eng.tensor_tensor(out=ybf[g][:], in0=t1[:, 0:T],
                                                  in1=t1[:, T:NT // 8], op=ALU.add))
                    return outs
                s_yt = sch.step("vector", ytail_fn, deps=half_steps + [s_bcast])
                prev[(c, 'scan', g)] = s_yt
                prev[(c, 'ytail', g)] = s_yt

                def xg_fn(eng, c=c, g=g, p=p):
                    outs = []
                    for k in range(8):
                        outs.append(eng.matmul(ps_xg[g][:],
                                               wu2[k][:, 128 * g:128 * (g + 1)],
                                               xnc[k][:], start=(k == 0), stop=(k == 7)))
                    return outs
                dep_xg = [s_xnload] + cvt[:16]
                if (c - 1, 'silu', g) in prev:
                    dep_xg.append(prev[(c - 1, 'silu', g)])
                s_xg = sch.step("tensor", xg_fn, deps=dep_xg)
                prev[(c, 'xg', g)] = s_xg

                def silu_fn(eng, c=c, g=g):
                    return [eng.activation(out=xg[g][:], in_=ps_xg[g][:], func=AF.Silu)]
                dep_si = [s_xg]
                if (c - 1, 'Y') in prev:
                    dep_si.append(prev[(c - 1, 'Y')])
                s_silu = sch.step("scalar", silu_fn, deps=dep_si)
                prev[(c, 'silu', g)] = s_silu

            def Y_fn(eng, c=c, p=p):
                outs = []
                for g in range(NG):
                    outs.append(eng.tensor_tensor(out=Yb[p][g][:], in0=ybf[g][:],
                                                  in1=xg[g][:], op=ALU.mult))
                return outs
            dep_Y = [prev[(c, 'scan', g)] for g in range(NG)]
            dep_Y += [prev[(c, 'silu', g)] for g in range(NG)]
            if c >= 2:
                dep_Y.append(prev[(c - 2, 'yp', 7)])
            s_Y = sch.step("vector", Y_fn, deps=dep_Y)
            prev[(c, 'Y')] = s_Y

            for cc in ([c - 1] if c >= 1 else []) + ([c] if c == NCH - 1 else []):
                pp2 = cc % 2
                for m in range(8):
                    def yp_fn(eng, cc=cc, m=m, pp2=pp2):
                        outs = []
                        for g in range(NG):
                            outs.append(eng.matmul(ps_yp[m % 2][:],
                                                   wdt[g][:, 128 * m:128 * (m + 1)],
                                                   Yb[pp2][g][:], start=(g == 0),
                                                   stop=(g == NG - 1)))
                        return outs
                    dep_yp = [prev[(cc, 'Y')]] + cvt[16:]
                    if (cc, 'evac', m - 2) in prev:
                        dep_yp.append(prev[(cc, 'evac', m - 2)])
                    elif cc >= 1 and m < 2:
                        dep_yp.append(prev[(cc - 1, 'evac', m + 6)])
                    s_yp = sch.step("tensor", yp_fn, deps=dep_yp)
                    prev[(cc, 'yp', m)] = s_yp

                    def ev_fn(eng, cc=cc, m=m):
                        return [eng.copy(out=evac[m % 2][:], in_=ps_yp[m % 2][:])]
                    dep_ev = [s_yp]
                    if (cc, 'store', m - 2) in prev:
                        dep_ev.append(prev[(cc, 'store', m - 2)])
                    elif cc >= 1 and m < 2:
                        dep_ev.append(prev[(cc - 1, 'store', m + 6)])
                    s_ev = sch.step("scalar", ev_fn, deps=dep_ev)
                    prev[(cc, 'evac', m)] = s_ev

                    @dmas(1)
                    def st_fn(eng, cc=cc, m=m):
                        return [eng.dma_start(
                            out=O_YP[128 * m:128 * (m + 1), cc * T:(cc + 1) * T],
                            in_=evac[m % 2][:])]
                    s_st = sch.dma("gpsimd", st_fn, deps=[s_ev])
                    prev[(cc, 'store', m)] = s_st
                    store_steps.append(s_st)

        def fin_fn(eng):
            return [eng.memset(rowa[0:1, 0:1], 0.0)]
        sch.step("vector", fin_fn, deps=store_steps)

        sch.emit(block)
    return nc


def build_L2():
    T = 512
    NCH = LTOT // T
    NG = DSL // 128
    nc = bass.Bass("TRN2", target_bir_lowering=False)
    DELTA = nc.dram_tensor("DELTA", [DSL, LTOT], F32, kind="ExternalInput").ap()
    XMS = nc.dram_tensor("XMS", [DSL, LTOT], BF16, kind="ExternalInput").ap()
    BFI = nc.dram_tensor("BFI", [NST, LTOT], BF16, kind="ExternalInput").ap()
    CFI = nc.dram_tensor("CFI", [NST, LTOT], BF16, kind="ExternalInput").ap()
    XNF = nc.dram_tensor("XNF", [DM, LTOT], BF16, kind="ExternalInput").ap()
    ALOG = nc.dram_tensor("ALOG", [DSL, NST], F32, kind="ExternalInput").ap()
    WU2T = nc.dram_tensor("WU2T", [DM, DSL], F32, kind="ExternalInput").ap()
    WDT = nc.dram_tensor("WDT", [DSL, DM], F32, kind="ExternalInput").ap()
    O_YP = nc.dram_tensor("O_YP", [DM, LTOT], F32, kind="ExternalOutput").ap()

    NT = NST * T

    with ExitStack() as ctx:
        e = ctx.enter_context
        wu2 = [e(nc.sbuf_tensor(f"wu2_{k}", [128, DSL], BF16)) for k in range(8)]
        wdt = [e(nc.sbuf_tensor(f"wdt_{g}", [128, DM], BF16)) for g in range(NG)]
        wtmp = [e(nc.sbuf_tensor(f"wtmp{i}", [128, DM // 2], F32)) for i in range(2)]
        alog = [e(nc.sbuf_tensor(f"alog{g}", [128, NST], F32)) for g in range(NG)]
        Aa = [e(nc.sbuf_tensor(f"Aa{g}", [128, NST], F32)) for g in range(NG)]
        iA = [e(nc.sbuf_tensor(f"iA{g}", [128, NST], F32)) for g in range(NG)]
        eAn = e(nc.sbuf_tensor("eAn", [128, NST], F32))
        hst = [e(nc.sbuf_tensor(f"hst{g}", [128, NST], F32)) for g in range(NG)]
        delta = [[e(nc.sbuf_tensor(f"delta{p}_{g}", [128, T], F32)) for g in range(NG)]
                 for p in range(2)]
        xmsb = [[e(nc.sbuf_tensor(f"xms{p}_{g}", [128, T], BF16)) for g in range(NG)]
                for p in range(2)]
        xnc = [e(nc.sbuf_tensor(f"xnc_{k}", [128, T], BF16)) for k in range(8)]
        bbc = e(nc.sbuf_tensor("bbc", [128, NT], BF16))
        cbc = e(nc.sbuf_tensor("cbc", [128, NT], BF16))
        abar = e(nc.sbuf_tensor("abar", [128, NT], F32))
        am1 = e(nc.sbuf_tensor("am1", [128, NT], BF16))
        uu = e(nc.sbuf_tensor("uu", [128, NT], BF16))
        hh = [e(nc.sbuf_tensor(f"hh{g}", [128, NT], BF16)) for g in range(NG)]
        qq = e(nc.sbuf_tensor("qq", [128, NT], BF16))
        t1 = e(nc.sbuf_tensor("t1", [128, NT // 2], BF16))
        t2 = e(nc.sbuf_tensor("t2", [128, NT // 4], BF16))
        ybf = [e(nc.sbuf_tensor(f"ybf{g}", [128, T], BF16)) for g in range(NG)]
        xg = [e(nc.sbuf_tensor(f"xg{g}", [128, T], BF16)) for g in range(NG)]
        Yb = [[e(nc.sbuf_tensor(f"Yb{p}_{g}", [128, T], BF16)) for g in range(NG)]
              for p in range(2)]

        evac = [e(nc.sbuf_tensor(f"evac{m}", [128, T], F32)) for m in range(2)]
        ps_xg = [e(nc.psum_tensor(f"ps_xg{g}", [128, T], F32)) for g in range(NG)]
        ps_yp = [e(nc.psum_tensor(f"ps_yp{i}", [128, T], F32)) for i in range(2)]

        sch = Sched(nc, ctx)
        block = e(nc.Block())

        @dmas(2)
        def alog_fn(eng):
            return [eng.dma_start(out=alog[g][:], in_=ALOG[128 * g:128 * (g + 1), :])
                    for g in range(NG)]
        s_alog = sch.dma("sync", alog_fn)

        aprep_steps = []
        for g in range(NG):
            def aexp_fn(eng, g=g):
                o1 = eng.activation(out=Aa[g][:], in_=alog[g][:], func=AF.Exp)
                o2 = eng.activation(out=eAn[:], in_=alog[g][:], func=AF.Exp, scale=-1.0)
                return [o1, o2]
            s_ae = sch.step("scalar", aexp_fn,
                            deps=[s_alog] + ([aprep_steps[-1]] if aprep_steps else []),
                            small=True)

            def aneg_fn(eng, g=g):
                o1 = eng.tensor_scalar(out=Aa[g][:], in0=Aa[g][:], scalar1=-1.0,
                                       scalar2=None, op0=ALU.mult)
                o2 = eng.tensor_scalar(out=iA[g][:], in0=eAn[:], scalar1=-1.0,
                                       scalar2=None, op0=ALU.mult)
                o3 = eng.memset(hst[g][:], 0.0)
                return [o1, o2, o3]
            s_an = sch.step("vector", aneg_fn, deps=[s_ae], small=True)
            aprep_steps.append(s_an)

        cvt = []
        i = 0
        for k in range(10):
            dst = wu2[k] if k < 8 else wdt[k - 8]
            w = DSL if k < 8 else DM
            SRC = WU2T if k < 8 else WDT
            row = 128 * k if k < 8 else 128 * (k - 8)
            for h in range(2):
                w2 = w // 2

                @dmas(1)
                def wl_fn(eng, SRC=SRC, row=row, h=h, w2=w2, i=i):
                    return [eng.dma_start(out=wtmp[i % 2][:, 0:w2],
                                          in_=SRC[row:row + 128, h * w2:(h + 1) * w2])]
                sw = sch.dma("sync", wl_fn, deps=[cvt[i - 2]] if i >= 2 else [])

                def cv_fn(eng, dst=dst, i=i, h=h, w2=w2):
                    return [eng.tensor_copy(out=dst[:, h * w2:(h + 1) * w2],
                                            in_=wtmp[i % 2][:, 0:w2])]
                sc = sch.step("vector", cv_fn, deps=[sw])
                cvt.append(sc)
                i += 1

        prev = {}
        store_steps = []
        for c in range(NCH):
            p = c % 2

            @dmas(2 * NG)
            def load_fn(eng, c=c, p=p):
                outs = []
                for g in range(NG):
                    outs.append(eng.dma_start(
                        out=delta[p][g][:],
                        in_=DELTA[128 * g:128 * (g + 1), c * T:(c + 1) * T]))
                    outs.append(eng.dma_start(
                        out=xmsb[p][g][:],
                        in_=XMS[128 * g:128 * (g + 1), c * T:(c + 1) * T]))
                return outs
            dep_l = []
            if c >= 2:
                dep_l = [prev[(c - 2, 'scan', NG - 1)]]
            s_load = sch.dma("sync", load_fn, deps=dep_l)

            @dmas(8)
            def xnload_fn(eng, c=c):
                return [eng.dma_start(
                    out=xnc[k][:],
                    in_=XNF[128 * k:128 * (k + 1), c * T:(c + 1) * T])
                    for k in range(8)]
            dep_xl = [prev[(c - 1, 'xg', NG - 1)]] if c >= 1 else []
            s_xnload = sch.dma("sync", xnload_fn, deps=dep_xl)

            @dmas(2)
            def bc_fn(eng, c=c):
                o1 = eng.dma_start(
                    out=bbc.ap().rearrange("p (n t) -> p n t", n=NST),
                    in_=BFI[:, c * T:(c + 1) * T].unsqueeze(0).broadcast_to([128, NST, T]))
                o2 = eng.dma_start(
                    out=cbc.ap().rearrange("p (n t) -> p n t", n=NST),
                    in_=CFI[:, c * T:(c + 1) * T].unsqueeze(0).broadcast_to([128, NST, T]))
                return [o1, o2]
            dep_b = [prev[(c - 1, 'scan', NG - 1)]] if c >= 1 else []
            s_bcast = sch.dma("sync", bc_fn, deps=dep_b)

            for g in range(NG):
                def abar_fn(eng, c=c, g=g, p=p):
                    outs = []
                    for n in range(NST):
                        outs.append(eng.activation(
                            out=abar[:, n * T:(n + 1) * T], in_=delta[p][g][:],
                            func=AF.Exp, scale=Aa[g][:, n:n + 1]))
                    return outs
                dep_a = [s_load, aprep_steps[g]]
                if (c, 'scan', g - 1) in prev:
                    dep_a.append(prev[(c, 'scan', g - 1)])
                elif (c - 1, 'scan', NG - 1) in prev:
                    dep_a.append(prev[(c - 1, 'scan', NG - 1)])
                s_abar = sch.step("scalar", abar_fn, deps=dep_a)
                prev[(c, 'abar', g)] = s_abar

                def scan_fn(eng, c=c, g=g, p=p):
                    outs = []
                    for n in range(NST):
                        outs.append(eng.tensor_scalar(
                            out=am1[:, n * T:(n + 1) * T],
                            in0=abar[:, n * T:(n + 1) * T],
                            scalar1=iA[g][:, n:n + 1], scalar2=iA[g][:, n:n + 1],
                            op0=ALU.mult, op1=ALU.subtract))
                    outs.append(eng.tensor_tensor(
                        out=qq.ap().rearrange("p (n t) -> p n t", n=NST),
                        in0=xmsb[p][g].ap().unsqueeze(1).broadcast_to([128, NST, T]),
                        in1=bbc.ap().rearrange("p (n t) -> p n t", n=NST),
                        op=ALU.mult))
                    outs.append(eng.tensor_tensor(out=uu[:], in0=am1[:], in1=qq[:],
                                                  op=ALU.mult))
                    for n in range(NST):
                        init = (hst[g][:, n:n + 1] if c == 0
                                else hh[g][:, (n + 1) * T - 1:(n + 1) * T])
                        outs.append(eng.tensor_tensor_scan(
                            out=hh[g][:, n * T:(n + 1) * T],
                            data0=abar[:, n * T:(n + 1) * T],
                            data1=uu[:, n * T:(n + 1) * T],
                            initial=init,
                            op0=ALU.mult, op1=ALU.add))
                    outs.append(eng.tensor_tensor(out=qq[:], in0=hh[g][:], in1=cbc[:],
                                                  op=ALU.mult))
                    outs.append(eng.tensor_tensor(out=t1[:], in0=qq[:, 0:NT // 2],
                                                  in1=qq[:, NT // 2:], op=ALU.add))
                    outs.append(eng.tensor_tensor(out=t2[:], in0=t1[:, 0:NT // 4],
                                                  in1=t1[:, NT // 4:], op=ALU.add))
                    outs.append(eng.tensor_tensor(out=t1[:, 0:NT // 8], in0=t2[:, 0:NT // 8],
                                                  in1=t2[:, NT // 8:], op=ALU.add))
                    outs.append(eng.tensor_tensor(out=ybf[g][:], in0=t1[:, 0:T],
                                                  in1=t1[:, T:NT // 8], op=ALU.add))
                    return outs
                s_scan = sch.step("vector", scan_fn, deps=[s_abar, s_bcast, s_load])
                prev[(c, 'scan', g)] = s_scan

                def xg_fn(eng, c=c, g=g, p=p):
                    outs = []
                    for k in range(8):
                        outs.append(eng.matmul(ps_xg[g][:],
                                               wu2[k][:, 128 * g:128 * (g + 1)],
                                               xnc[k][:], start=(k == 0), stop=(k == 7)))
                    return outs
                dep_xg = [s_xnload] + cvt[:16]
                if (c - 1, 'silu', g) in prev:
                    dep_xg.append(prev[(c - 1, 'silu', g)])
                s_xg = sch.step("tensor", xg_fn, deps=dep_xg)
                prev[(c, 'xg', g)] = s_xg

                def silu_fn(eng, c=c, g=g):
                    return [eng.activation(out=xg[g][:], in_=ps_xg[g][:], func=AF.Silu)]
                dep_si = [s_xg]
                if (c - 1, 'Y') in prev:
                    dep_si.append(prev[(c - 1, 'Y')])
                s_silu = sch.step("scalar", silu_fn, deps=dep_si)
                prev[(c, 'silu', g)] = s_silu

            def Y_fn(eng, c=c, p=p):
                outs = []
                for g in range(NG):
                    outs.append(eng.tensor_tensor(out=Yb[p][g][:], in0=ybf[g][:],
                                                  in1=xg[g][:], op=ALU.mult))
                return outs
            dep_Y = [prev[(c, 'scan', g)] for g in range(NG)]
            dep_Y += [prev[(c, 'silu', g)] for g in range(NG)]
            if c >= 2:
                dep_Y.append(prev[(c - 2, 'yp', 7)])
            s_Y = sch.step("vector", Y_fn, deps=dep_Y)
            prev[(c, 'Y')] = s_Y

            for m in range(8):
                def yp_fn(eng, c=c, m=m, p=p):
                    outs = []
                    for g in range(NG):
                        outs.append(eng.matmul(ps_yp[m % 2][:],
                                               wdt[g][:, 128 * m:128 * (m + 1)],
                                               Yb[p][g][:], start=(g == 0),
                                               stop=(g == NG - 1)))
                    return outs
                dep_yp = [s_Y] + cvt[16:]
                if (c, 'evac', m - 2) in prev:
                    dep_yp.append(prev[(c, 'evac', m - 2)])
                elif c >= 1 and m < 2:
                    dep_yp.append(prev[(c - 1, 'evac', m + 6)])
                s_yp = sch.step("tensor", yp_fn, deps=dep_yp)
                prev[(c, 'yp', m)] = s_yp

                def ev_fn(eng, c=c, m=m):
                    return [eng.copy(out=evac[m % 2][:], in_=ps_yp[m % 2][:])]
                dep_ev = [s_yp]
                if (c, 'store', m - 2) in prev:
                    dep_ev.append(prev[(c, 'store', m - 2)])
                elif c >= 1 and m < 2:
                    dep_ev.append(prev[(c - 1, 'store', m + 6)])
                s_ev = sch.step("scalar", ev_fn, deps=dep_ev)
                prev[(c, 'evac', m)] = s_ev

                @dmas(1)
                def st_fn(eng, c=c, m=m):
                    return [eng.dma_start(
                        out=O_YP[128 * m:128 * (m + 1), c * T:(c + 1) * T],
                        in_=evac[m % 2][:])]
                s_st = sch.dma("gpsimd", st_fn, deps=[s_ev])
                prev[(c, 'store', m)] = s_st
                store_steps.append(s_st)

        def fin_fn(eng):
            return [eng.memset(eAn[0:1, 0:1], 0.0)]
        sch.step("vector", fin_fn, deps=store_steps)

        sch.emit(block)
    return nc


def build_L3():
    T = 256
    NCH = TC // T
    HID = 3072
    NH = HID // 128
    nc = bass.Bass("TRN2", target_bir_lowering=False)
    X = nc.dram_tensor("X", [DM, TC], F32, kind="ExternalInput").ap()
    YP = nc.dram_tensor("YP", [NCORE * DM, TC], F32, kind="ExternalInput").ap()
    BD = nc.dram_tensor("BD", [128, 8], F32, kind="ExternalInput").ap()
    WGT = nc.dram_tensor("WGT", [DM, HID], I32, kind="ExternalInput").ap()
    WUT = nc.dram_tensor("WUT", [DM, HID], I32, kind="ExternalInput").ap()
    WDT3 = nc.dram_tensor("WDT3", [HID, DM], I32, kind="ExternalInput").ap()
    O_OUT = nc.dram_tensor("O_OUT", [DM, TC], F32, kind="ExternalOutput").ap()

    with ExitStack() as ctx:
        e = ctx.enter_context
        wg = [e(nc.sbuf_tensor(f"wg{k}", [128, HID], BF16)) for k in range(8)]
        wu = [e(nc.sbuf_tensor(f"wu{k}", [128, HID], BF16)) for k in range(8)]
        wd = [e(nc.sbuf_tensor(f"wd{k}", [128, DM], BF16)) for k in range(NH)]
        wtmp = [e(nc.sbuf_tensor(f"wtmp{i}", [128, HID // 2], I32)) for i in range(2)]
        bd = e(nc.sbuf_tensor("bd", [128, 8], F32))
        ones1 = e(nc.sbuf_tensor("ones1", [128, 1], F32))
        onesr = e(nc.sbuf_tensor("onesr", [1, 128], F32))
        cbank = e(nc.sbuf_tensor("cbank", [128, 4], F32))
        xc = [e(nc.sbuf_tensor(f"xc{g}", [128, T], F32)) for g in range(8)]
        xn = [e(nc.sbuf_tensor(f"xn{g}", [128, T], F32)) for g in range(8)]
        xnb = [e(nc.sbuf_tensor(f"xnb{g}", [128, T], BF16)) for g in range(8)]
        sq = xn
        tmpA = e(nc.sbuf_tensor("tmpA", [128, T], F32))
        rowa = e(nc.sbuf_tensor("rowa", [1, T], F32))
        rowb = e(nc.sbuf_tensor("rowb", [1, T], F32))
        rowc = e(nc.sbuf_tensor("rowc", [1, T], F32))
        gsil = e(nc.sbuf_tensor("gsil", [128, T], BF16))
        hply = [e(nc.sbuf_tensor(f"hply{k}", [128, T], BF16)) for k in range(NH)]
        ypt = [e(nc.sbuf_tensor(f"ypt{i}", [128, T], F32)) for i in range(8)]
        acc = e(nc.sbuf_tensor("acc", [128, T], F32))
        outt = [e(nc.sbuf_tensor(f"outt{i}", [128, T], F32)) for i in range(2)]

        ps_s1 = e(nc.psum_tensor("ps_s1", [1, T], F32))
        ps_s2 = e(nc.psum_tensor("ps_s2", [1, T], F32))
        ps_r = e(nc.psum_tensor("ps_r", [128, T], F32))
        ps_m = e(nc.psum_tensor("ps_m", [128, T], F32))
        ps_g = e(nc.psum_tensor("ps_g", [128, T], F32))
        ps_u = e(nc.psum_tensor("ps_u", [128, T], F32))
        ps_d = [e(nc.psum_tensor(f"ps_d{i}", [128, T], F32)) for i in range(2)]

        sch = Sched(nc, ctx)
        block = e(nc.Block())
        def ones_fn(eng):
            return [eng.memset(ones1[:], 1.0), eng.memset(onesr[:], 1.0),
                    eng.memset(cbank[:, 0:1], float(EPS)),
                    eng.memset(cbank[:, 1:2], -0.5)]
        s_ones = sch.step("vector", ones_fn, small=True)
        P = dict(sq=sq, rowa=rowa, rowb=rowb, rowc=rowc, ones1=ones1,
                 onesr=onesr, ps_s1=ps_s1, ps_s2=ps_s2, ps_r=ps_r, ps_m=ps_m,
                 cb_eps=cbank[0:1, 0:1], cb_mhalf=cbank[0:1, 1:2],
                 cb_dep=[s_ones])

        @dmas(1)
        def bd_fn(eng):
            return [eng.dma_start(out=bd[:], in_=BD)]
        s_bd = sch.dma("sync", bd_fn)

        cvt = []
        srcs = ([(WGT, wg[k], HID, k) for k in range(8)]
                + [(WUT, wu[k], HID, k) for k in range(8)]
                + [(WDT3, wd[k], DM, k) for k in range(NH)])
        i = 0
        for (SRC, dst, w, k) in srcs:
            for h in range(2):
                w2 = w // 2

                @dmas(1)
                def wl_fn(eng, SRC=SRC, k=k, w2=w2, h=h, i=i):
                    return [eng.dma_start(
                        out=wtmp[i % 2][:, 0:w2],
                        in_=SRC[128 * k:128 * (k + 1), h * w2:(h + 1) * w2])]
                sw = sch.dma("sync", wl_fn, deps=[cvt[i - 2]] if i >= 2 else [])

                def cv_fn(eng, dst=dst, i=i, w2=w2, h=h):
                    return [eng.tensor_scalar(
                        out=dst[:, h * w2:(h + 1) * w2], in0=wtmp[i % 2][:, 0:w2],
                        scalar1=float(1.0 / 127.0), scalar2=None, op0=ALU.mult)]
                sc = sch.step("vector", cv_fn, deps=[sw])
                cvt.append(sc)
                i += 1

        prev = {}
        store_steps = []
        for c in range(NCH):
            @dmas(8)
            def xl_fn(eng, c=c):
                return [eng.dma_start(out=xc[g][:],
                                      in_=X[128 * g:128 * (g + 1), c * T:(c + 1) * T])
                        for g in range(8)]
            dep_x = [prev[(c - 1, 'fin', 7)]] if c >= 1 else []
            s_xl = sch.dma("sync", xl_fn, deps=dep_x)

            s_bc1, s_sq1 = emit_ln(sch, P, xc, 8, T, [s_xl])

            def xn_fn(eng, c=c):
                outs = []
                for g in range(8):
                    outs.append(eng.tensor_tensor(out=tmpA[:], in0=xc[g][:],
                                                  in1=ps_r[:], op=ALU.mult))
                    outs.append(eng.tensor_tensor(out=xn[g][:], in0=tmpA[:],
                                                  in1=ps_m[:], op=ALU.subtract))
                    outs.append(eng.tensor_copy(out=xnb[g][:], in_=xn[g][:]))
                return outs
            dep_xn = [s_bc1, s_sq1]
            if c >= 1:
                dep_xn.append(prev[(c - 1, 'ffn_mm_last')])
                dep_xn.append(prev[(c - 1, 'out', 7)])
            s_xn = sch.step("vector", xn_fn, deps=dep_xn, small=True)
            prev[(c, 'xn')] = s_xn

            for k in range(NH):
                def gu_fn(eng, c=c, k=k):
                    outs = []
                    for j in range(8):
                        outs.append(eng.matmul(ps_g[:], wg[j][:, 128 * k:128 * (k + 1)],
                                               xnb[j][:], start=(j == 0), stop=(j == 7)))
                    for j in range(8):
                        outs.append(eng.matmul(ps_u[:], wu[j][:, 128 * k:128 * (k + 1)],
                                               xnb[j][:], start=(j == 0), stop=(j == 7)))
                    return outs
                dep_g = [s_xn] + (cvt[:32] if c == 0 and k == 0 else [])
                if (c, 'h', k - 1) in prev:
                    dep_g.append(prev[(c, 'h', k - 1)])
                elif c >= 1 and k == 0:
                    dep_g.append(prev[(c - 1, 'h', NH - 1)])
                s_gu = sch.step("tensor", gu_fn, deps=dep_g)

                def sil_fn(eng, c=c, k=k):
                    return [eng.activation(out=gsil[:], in_=ps_g[:], func=AF.Silu)]
                s_sil = sch.step("scalar", sil_fn, deps=[s_gu], small=True)

                def h_fn(eng, c=c, k=k):
                    return [eng.tensor_tensor(out=hply[k][:], in0=gsil[:],
                                              in1=ps_u[:], op=ALU.mult)]
                dep_h = [s_sil]
                if (c - 1, 'down_done') in prev and c >= 1:
                    dep_h.append(prev[(c - 1, 'down_done')])
                s_h = sch.step("vector", h_fn, deps=dep_h, small=True)
                prev[(c, 'h', k)] = s_h

            prev[(c, 'ffn_mm_last')] = prev[(c, 'h', NH - 1)]

            h_all = [prev[(c, 'h', k)] for k in range(NH)]
            down_mm_steps = []
            for m in range(8):
                def dn_fn(eng, c=c, m=m):
                    outs = []
                    for k in range(NH):
                        outs.append(eng.matmul(ps_d[m % 2][:],
                                               wd[k][:, 128 * m:128 * (m + 1)],
                                               hply[k][:], start=(k == 0),
                                               stop=(k == NH - 1)))
                    return outs
                dep_d = h_all + (cvt[32:] if c == 0 and m == 0 else [])
                if (c, 'fin', m - 2) in prev:
                    dep_d.append(prev[(c, 'fin', m - 2)])
                elif c >= 1 and m < 2:
                    dep_d.append(prev[(c - 1, 'fin', m + 6)])
                s_dn = sch.step("tensor", dn_fn, deps=dep_d)
                down_mm_steps.append(s_dn)

                @dmas(8)
                def ypl2_fn(eng, c=c, m=m):
                    return [eng.dma_start(
                        out=ypt[r][:],
                        in_=YP[r * DM + 128 * m:r * DM + 128 * (m + 1),
                               c * T:(c + 1) * T]) for r in range(8)]
                dep_yl = [prev[(c, 'fin', m - 1)]] if (c, 'fin', m - 1) in prev else \
                         ([prev[(c - 1, 'fin', 7)]] if c >= 1 else [])
                s_yl = sch.dma("gpsimd", ypl2_fn, deps=dep_yl)

                def fin_fn(eng, c=c, m=m):
                    outs = []
                    outs.append(eng.tensor_tensor(out=acc[:], in0=xc[m][:],
                                                  in1=xn[m][:], op=ALU.add))
                    outs.append(eng.tensor_tensor(out=acc[:], in0=acc[:],
                                                  in1=ps_d[m % 2][:], op=ALU.add))
                    outs.append(eng.tensor_scalar(out=acc[:], in0=acc[:],
                                                  scalar1=bd[:, m:m + 1], scalar2=None,
                                                  op0=ALU.add))
                    for r in range(7):
                        outs.append(eng.tensor_tensor(out=acc[:], in0=acc[:],
                                                      in1=ypt[r][:], op=ALU.add))
                    outs.append(eng.tensor_tensor(out=outt[m % 2][:], in0=acc[:],
                                                  in1=ypt[7][:], op=ALU.add))
                    return outs
                dep_f = [s_dn, s_yl, s_xn]
                if (c, 'store', m - 2) in prev:
                    dep_f.append(prev[(c, 'store', m - 2)])
                elif c >= 1 and m < 2:
                    dep_f.append(prev[(c - 1, 'store', m + 6)])
                s_fin = sch.step("vector", fin_fn, deps=dep_f, small=True)
                prev[(c, 'fin', m)] = s_fin

                @dmas(1)
                def ost_fn(eng, c=c, m=m):
                    return [eng.dma_start(
                        out=O_OUT[128 * m:128 * (m + 1), c * T:(c + 1) * T],
                        in_=outt[m % 2][:])]
                s_ost = sch.dma("gpsimd", ost_fn, deps=[s_fin])
                prev[(c, 'store', m)] = s_ost
                prev[(c, 'out', m)] = s_ost
                store_steps.append(s_ost)
            prev[(c, 'down_done')] = down_mm_steps[-1]

        def fin2_fn(eng):
            return [eng.memset(rowa[0:1, 0:1], 0.0)]
        sch.step("vector", fin2_fn, deps=store_steps)

        sch.emit(block)
    return nc


_cache = {}


def _get(name, builder):
    if name not in _cache:
        _cache[name] = builder()
    return _cache[name]


def kernel(L0_attn_X, L0_attn_ln_g, L0_attn_ln_b, L0_attn_W_up1, L0_attn_conv_w,
           L0_attn_conv_b, L0_attn_W_ll, L0_attn_b_ll, L0_attn_A_log,
           L0_attn_W_up2, L0_attn_W_down, L0_attn_b_down, L0_ffn_w_gate,
           L0_ffn_w_up, L0_ffn_w_down):
    f32 = np.float32
    xT = np.ascontiguousarray(np.asarray(L0_attn_X, f32)[0].T)
    wu1T = np.ascontiguousarray(np.asarray(L0_attn_W_up1, f32).T)
    wllT = np.ascontiguousarray(np.asarray(L0_attn_W_ll, f32).T)
    wu2T = np.ascontiguousarray(np.asarray(L0_attn_W_up2, f32).T)
    wdnT = np.ascontiguousarray(np.asarray(L0_attn_W_down, f32).T)
    cwr = np.ascontiguousarray(
        np.asarray(L0_attn_conv_w, f32)[:, 0, :].reshape(16, 128, 4)
        .transpose(1, 0, 2).reshape(128, 64))
    cbr = np.ascontiguousarray(np.asarray(L0_attn_conv_b, f32).reshape(16, 128).T)
    lngr = np.ascontiguousarray(np.asarray(L0_attn_ln_g, f32).reshape(8, 128).T)
    lnbr = np.ascontiguousarray(np.asarray(L0_attn_ln_b, f32).reshape(8, 128).T)
    bll_pad = np.zeros((17 * 128,), f32)
    bll_pad[:EOUT] = np.asarray(L0_attn_b_ll, f32)
    bllr = np.ascontiguousarray(bll_pad.reshape(17, 128).T)
    bdr = np.ascontiguousarray(np.asarray(L0_attn_b_down, f32).reshape(8, 128).T)
    wgT = np.ascontiguousarray(np.asarray(L0_ffn_w_gate, np.int32).T)
    wuT = np.ascontiguousarray(np.asarray(L0_ffn_w_up, np.int32).T)
    wdT = np.ascontiguousarray(np.asarray(L0_ffn_w_down, np.int32).T)
    alog = np.ascontiguousarray(np.asarray(L0_attn_A_log, f32))

    nc1 = _get("L1", build_L1)
    in1 = []
    for i in range(NCORE):
        lo = i * TC
        xsl = np.zeros((DM, TC + 3), f32)
        xsl[:, 3:] = xT[:, lo:lo + TC]
        if i > 0:
            xsl[:, 0:3] = xT[:, lo - 3:lo]
        hmv = np.zeros((1, 3), f32) if i == 0 else np.ones((1, 3), f32)
        in1.append(dict(X=xsl, HM=hmv, LNG=lngr, LNB=lnbr, WU1T=wu1T,
                        WLLT=wllT, CW=cwr, CB=cbr, BLL=bllr))
    r1 = run_bass_kernel_spmd(nc1, in1, core_ids=list(range(NCORE))).results

    delta_full = np.concatenate([np.asarray(r["O_DELTA"]) for r in r1], axis=1)
    xms_full = np.concatenate([np.asarray(r["O_XMS"]).view(BFNP)
                               if np.asarray(r["O_XMS"]).dtype != BFNP
                               else np.asarray(r["O_XMS"]) for r in r1], axis=1)
    bc_full = np.concatenate([np.asarray(r["O_BC"]).view(BFNP)
                              if np.asarray(r["O_BC"]).dtype != BFNP
                              else np.asarray(r["O_BC"]) for r in r1], axis=1)
    xn_full = np.concatenate([np.asarray(r["O_XN"]).view(BFNP)
                              if np.asarray(r["O_XN"]).dtype != BFNP
                              else np.asarray(r["O_XN"]) for r in r1], axis=1)

    nc2 = _get("L2", build_L2)
    in2 = []
    for j in range(NCORE):
        sl = slice(j * DSL, (j + 1) * DSL)
        in2.append(dict(
            DELTA=np.ascontiguousarray(delta_full[sl]),
            XMS=np.ascontiguousarray(xms_full[sl]),
            BFI=np.ascontiguousarray(bc_full[0:NST]),
            CFI=np.ascontiguousarray(bc_full[NST:2 * NST]),
            XNF=xn_full,
            ALOG=np.ascontiguousarray(alog[sl]),
            WU2T=np.ascontiguousarray(wu2T[:, sl]),
            WDT=np.ascontiguousarray(wdnT[sl]),
        ))
    r2 = run_bass_kernel_spmd(nc2, in2, core_ids=list(range(NCORE))).results
    yp_all = [np.asarray(r["O_YP"]) for r in r2]

    nc3 = _get("L3", build_L3)
    in3 = []
    for i in range(NCORE):
        lo = i * TC
        ypstack = np.ascontiguousarray(
            np.concatenate([yp[:, lo:lo + TC] for yp in yp_all], axis=0))
        in3.append(dict(
            X=np.ascontiguousarray(xT[:, lo:lo + TC]),
            YP=ypstack, BD=bdr, WGT=wgT, WUT=wuT, WDT3=wdT,
        ))
    r3 = run_bass_kernel_spmd(nc3, in3, core_ids=list(range(NCORE))).results
    out = np.concatenate([np.asarray(r["O_OUT"]) for r in r3], axis=1)
    return np.ascontiguousarray(out.T).reshape(1, LTOT, DM).astype(np.float32)


# version 07 (final)
# speedup vs baseline: 1.3711x; 1.3711x over previous
import numpy as np
import ml_dtypes
from contextlib import ExitStack

import concourse.bass as bass
import concourse.mybir as mybir
from concourse.bass_utils import run_bass_kernel_spmd

F32 = mybir.dt.float32
BF16 = mybir.dt.bfloat16
I32 = mybir.dt.int32
AF = mybir.ActivationFunctionType
ALU = mybir.AluOpType
BFNP = ml_dtypes.bfloat16

NCORE = 8
DM = 1024
DSS = 2048
NST = 16
EOUT = 2080
LTOT = 16384
TC = LTOT // NCORE
DSL = DSS // NCORE
EPS = 1e-5
L3_DEBUG_FFN_ONLY = False
L3_DEBUG_NO_YP = False
L3_DEBUG_YP_ONLY = False


ENGINES = ("sync", "vector", "scalar", "tensor", "gpsimd")


class Step:
    __slots__ = ("eng", "fn", "deps", "kind", "post", "idx", "small", "dpost")

    def __init__(self, eng, fn, deps, kind, small=False):
        self.eng = eng
        self.fn = fn
        self.deps = deps
        self.kind = kind
        self.post = None
        self.idx = None
        self.small = small


class Sched:
    def __init__(self, nc, ctx):
        self.nc = nc
        self.steps = {e: [] for e in ENGINES}
        self.sems = {e: ctx.enter_context(nc.semaphore(f"sem_{e}")) for e in ENGINES}
        self.dsems = {e: ctx.enter_context(nc.semaphore(f"dsem_{e}")) for e in ENGINES}

    def step(self, eng, fn, deps=(), kind="compute", small=False):
        s = Step(eng, fn, list(deps), kind, small=small)
        s.idx = len(self.steps[eng])
        self.steps[eng].append(s)
        return s

    def dma(self, eng, fn, deps=()):
        return self.step(eng, fn, deps, kind="dma")

    def emit(self, block):
        for e in ENGINES:
            cumd = 0
            cumc = 0
            for s in self.steps[e]:
                cumc += 1
                s.post = cumc
                if s.kind == "dma":
                    cumd += 16 * getattr(s.fn, "n_dma")
                    s.dpost = cumd

        sched = self

        def body_for(eng_name):
            def body(eng):
                waited = {e: 0 for e in ENGINES}
                for s in sched.steps[eng_name]:
                    for d in s.deps:
                        if d.eng == eng_name:
                            continue
                        if waited[d.eng] < d.post:
                            eng.wait_ge(sched.sems[d.eng], d.post)
                            waited[d.eng] = d.post
                    instrs = s.fn(eng)
                    if s.kind == "dma":
                        for ins in instrs:
                            ins.then_inc(sched.dsems[eng_name], 16)
                        eng.wait_ge(sched.dsems[eng_name], s.dpost)
                        eng.sem_inc(sched.sems[eng_name], 1)
                    elif s.small:
                        eng.drain().then_inc(sched.sems[eng_name], 1)
                    else:
                        instrs[-1].then_inc(sched.sems[eng_name], 1)
            return body

        block.sync(body_for("sync"))
        block.vector(body_for("vector"))
        block.scalar(body_for("scalar"))
        block.tensor(body_for("tensor"))
        block.gpsimd(body_for("gpsimd"))


def dmas(n):
    def wrap(fn):
        fn.n_dma = n
        return fn
    return wrap


def emit_ln(sch, p, xg, ng, T, dep_x, dep_sums=()):
    nc = sch.nc

    def sq_fn(eng):
        out = []
        for g in range(ng):
            out.append(eng.activation(out=p["sq"][g][:, 0:T], in_=xg[g][:, 0:T],
                                      func=AF.Square))
        return out
    s_sq = sch.step("scalar", sq_fn, deps=list(dep_x))

    def sums_fn(eng):
        out = []
        for g in range(ng):
            out.append(eng.matmul(p["ps_s1"][0:1, 0:T], p["ones1"][:],
                                  xg[g][:, 0:T], start=(g == 0), stop=(g == ng - 1)))
        for g in range(ng):
            out.append(eng.matmul(p["ps_s2"][0:1, 0:T], p["ones1"][:],
                                  p["sq"][g][:, 0:T], start=(g == 0), stop=(g == ng - 1)))
        return out
    s_sums = sch.step("tensor", sums_fn, deps=list(dep_x) + [s_sq] + list(dep_sums))

    def rows_fn(eng):
        o1 = eng.tensor_scalar(out=p["rowa"][:, 0:T], in0=p["ps_s1"][0:1, 0:T],
                               scalar1=1.0 / DM, scalar2=None, op0=ALU.mult)
        o2 = eng.tensor_scalar(out=p["rowb"][:, 0:T], in0=p["ps_s2"][0:1, 0:T],
                               scalar1=1.0 / DM, scalar2=None, op0=ALU.mult)
        o3 = eng.tensor_tensor(out=p["rowc"][:, 0:T], in0=p["rowa"][:, 0:T],
                               in1=p["rowa"][:, 0:T], op=ALU.mult)
        o4 = eng.tensor_tensor(out=p["rowb"][:, 0:T], in0=p["rowb"][:, 0:T],
                               in1=p["rowc"][:, 0:T], op=ALU.subtract)
        return [o1, o2, o3, o4]
    s_rows = sch.step("vector", rows_fn, deps=[s_sums])

    def rstd_fn(eng):
        o1 = eng.activation(out=p["rowc"][:, 0:T], in_=p["rowb"][:, 0:T],
                            func=AF.Ln, bias=p["cb_eps"])
        o2 = eng.activation(out=p["rowb"][:, 0:T], in_=p["rowc"][:, 0:T],
                            func=AF.Exp, scale=p["cb_mhalf"])
        return [o1, o2]
    s_rstd = sch.step("scalar", rstd_fn, deps=[s_rows] + list(p.get("cb_dep", [])))

    def mursd_fn(eng):
        return [eng.tensor_tensor(out=p["rowc"][:, 0:T], in0=p["rowa"][:, 0:T],
                                  in1=p["rowb"][:, 0:T], op=ALU.mult)]
    s_mursd = sch.step("vector", mursd_fn, deps=[s_rstd])

    def bc_fn(eng):
        o1 = eng.matmul(p["ps_r"][:, 0:T], p["onesr"][:], p["rowb"][:, 0:T],
                        start=True, stop=True)
        o2 = eng.matmul(p["ps_m"][:, 0:T], p["onesr"][:], p["rowc"][:, 0:T],
                        start=True, stop=True)
        return [o1, o2]
    s_bc = sch.step("tensor", bc_fn, deps=[s_mursd, s_rstd])
    return s_bc, s_sq


def build_L1():
    T = 512
    NCH = TC // T
    nc = bass.Bass("TRN2", target_bir_lowering=False)
    X = nc.dram_tensor("X", [DM, TC + 3], F32, kind="ExternalInput").ap()
    HM = nc.dram_tensor("HM", [1, 3], F32, kind="ExternalInput").ap()
    LNG = nc.dram_tensor("LNG", [128, 8], F32, kind="ExternalInput").ap()
    LNB = nc.dram_tensor("LNB", [128, 8], F32, kind="ExternalInput").ap()
    WU1T = nc.dram_tensor("WU1T", [DM, DSS], F32, kind="ExternalInput").ap()
    WLLT = nc.dram_tensor("WLLT", [DSS, EOUT], F32, kind="ExternalInput").ap()
    CW = nc.dram_tensor("CW", [128, 64], F32, kind="ExternalInput").ap()
    CB = nc.dram_tensor("CB", [128, 16], F32, kind="ExternalInput").ap()
    BLL = nc.dram_tensor("BLL", [128, 17], F32, kind="ExternalInput").ap()

    O_DELTA = nc.dram_tensor("O_DELTA", [DSS, TC], F32, kind="ExternalOutput").ap()
    O_XMS = nc.dram_tensor("O_XMS", [DSS, TC], BF16, kind="ExternalOutput").ap()
    O_BC = nc.dram_tensor("O_BC", [32, TC], BF16, kind="ExternalOutput").ap()
    O_XN = nc.dram_tensor("O_XN", [DM, TC], BF16, kind="ExternalOutput").ap()
    O_HS = nc.dram_tensor("O_HS", [128, 48], F32, kind="ExternalOutput").ap()
    O_XNHB = nc.dram_tensor("O_XNHB", [128, 24], BF16, kind="ExternalOutput").ap()

    with ExitStack() as ctx:
        e = ctx.enter_context
        wu1 = [e(nc.sbuf_tensor(f"wu1_{k}", [128, DSS], BF16)) for k in range(8)]
        wll = [e(nc.sbuf_tensor(f"wll_{k}", [128, EOUT], BF16)) for k in range(16)]
        wtmp = [e(nc.sbuf_tensor(f"wtmp{i}", [128, EOUT], F32)) for i in range(2)]
        lng = e(nc.sbuf_tensor("lng", [128, 8], F32))
        lnb = e(nc.sbuf_tensor("lnb", [128, 8], F32))
        cw = e(nc.sbuf_tensor("cw", [128, 64], F32))
        cb = e(nc.sbuf_tensor("cb", [128, 16], F32))
        bll = e(nc.sbuf_tensor("bll", [128, 17], F32))
        hm = e(nc.sbuf_tensor("hm", [128, 3], F32))
        ones1 = e(nc.sbuf_tensor("ones1", [128, 1], F32))
        onesr = e(nc.sbuf_tensor("onesr", [1, 128], F32))
        cbank = e(nc.sbuf_tensor("cbank", [128, 4], F32))
        xc = [e(nc.sbuf_tensor(f"xc{g}", [128, T], F32)) for g in range(8)]
        xh = e(nc.sbuf_tensor("xh", [128, 24], F32))
        xnhb = e(nc.sbuf_tensor("xnhb", [128, 24], BF16))
        xnbf = [e(nc.sbuf_tensor(f"xnbf{g}", [128, T], BF16)) for g in range(8)]
        tmpA = e(nc.sbuf_tensor("tmpA", [128, T], F32))
        tmpB = e(nc.sbuf_tensor("tmpB", [128, T], F32))
        sq = [e(nc.sbuf_tensor(f"sq{g}", [128, T], F32)) for g in range(8)]
        rowa = e(nc.sbuf_tensor("rowa", [1, T], F32))
        rowb = e(nc.sbuf_tensor("rowb", [1, T], F32))
        rowc = e(nc.sbuf_tensor("rowc", [1, T], F32))
        convb = e(nc.sbuf_tensor("convb", [128, T + 3], F32))
        hsave = e(nc.sbuf_tensor("hsave", [128, 48], F32))
        ct = [e(nc.sbuf_tensor(f"ct{i}", [128, T], F32)) for i in range(4)]
        cs = [e(nc.sbuf_tensor(f"cs{i}", [128, T], F32)) for i in range(2)]
        xms = [e(nc.sbuf_tensor(f"xms{g}", [128, T], BF16)) for g in range(16)]
        zz = e(nc.sbuf_tensor("zz", [128, T], F32))
        az = e(nc.sbuf_tensor("az", [128, T], F32))
        lz = e(nc.sbuf_tensor("lz", [128, T], F32))
        mz = e(nc.sbuf_tensor("mz", [128, T], F32))
        dl = [e(nc.sbuf_tensor(f"dl{i}", [128, T], F32)) for i in range(2)]
        bcb = e(nc.sbuf_tensor("bcb", [32, T], BF16))

        ps_s1 = e(nc.psum_tensor("ps_s1", [1, T], F32))
        ps_s2 = e(nc.psum_tensor("ps_s2", [1, T], F32))
        ps_r = e(nc.psum_tensor("ps_r", [128, T], F32))
        ps_m = e(nc.psum_tensor("ps_m", [128, T], F32))
        ps_xm = [e(nc.psum_tensor(f"ps_xm{i}", [128, T], F32)) for i in range(2)]
        ps_pp = [e(nc.psum_tensor(f"ps_pp{i}", [128, T], F32)) for i in range(2)]

        sch = Sched(nc, ctx)
        block = e(nc.Block())
        def ones_fn(eng):
            o1 = eng.memset(ones1[:], 1.0)
            o2 = eng.memset(onesr[:], 1.0)
            o3 = eng.memset(cbank[:, 0:1], float(EPS))
            o4 = eng.memset(cbank[:, 1:2], -0.5)
            o5 = eng.memset(cbank[:, 2:3], 1.0)
            return [o1, o2, o3, o4, o5]
        s_ones = sch.step("vector", ones_fn, small=True)
        P = dict(sq=sq, rowa=rowa, rowb=rowb, rowc=rowc, ones1=ones1,
                 onesr=onesr, ps_s1=ps_s1, ps_s2=ps_s2, ps_r=ps_r, ps_m=ps_m,
                 cb_eps=cbank[0:1, 0:1], cb_mhalf=cbank[0:1, 1:2],
                 cb_dep=[s_ones])

        @dmas(7)
        def consts_fn(eng):
            return [
                eng.dma_start(out=lng[:], in_=LNG),
                eng.dma_start(out=lnb[:], in_=LNB),
                eng.dma_start(out=cw[:], in_=CW),
                eng.dma_start(out=cb[:], in_=CB),
                eng.dma_start(out=bll[:], in_=BLL),
                eng.dma_start(out=hm[:], in_=HM.broadcast_to([128, 3])),
                eng.dma_start(out=xh.ap().rearrange("p (g t) -> p g t", g=8),
                              in_=X.rearrange("(g p) t -> p g t", p=128)[:, :, 0:3]),
            ]
        s_consts = sch.dma("sync", consts_fn)

        wsteps = []
        cvt_steps = []
        for k in range(24):
            dst = wu1[k] if k < 8 else wll[k - 8]
            src = (WU1T[128 * k:128 * (k + 1), :] if k < 8
                   else WLLT[128 * (k - 8):128 * (k - 7), :])
            w = DSS if k < 8 else EOUT
            deps = [cvt_steps[k - 2]] if k >= 2 else []

            @dmas(1)
            def wl_fn(eng, src=src, k=k, w=w):
                return [eng.dma_start(out=wtmp[k % 2][:, 0:w], in_=src)]
            sw = sch.dma("sync", wl_fn, deps=deps)
            wsteps.append(sw)

            def cv_fn(eng, dst=dst, k=k, w=w):
                return [eng.tensor_copy(out=dst[:, 0:w], in_=wtmp[k % 2][:, 0:w])]
            sc = sch.step("vector", cv_fn, deps=[sw])
            cvt_steps.append(sc)

        def hsq_fn(eng):
            return [eng.activation(out=tmpA[:, 0:24], in_=xh[:], func=AF.Square)]
        s_hsq = sch.step("scalar", hsq_fn, deps=[s_consts])

        def hsums_fn(eng):
            outs = []
            for g in range(8):
                outs.append(eng.matmul(ps_s1[0:1, 0:3], ones1[:],
                                       xh[:, 3 * g:3 * g + 3],
                                       start=(g == 0), stop=(g == 7)))
            for g in range(8):
                outs.append(eng.matmul(ps_s2[0:1, 0:3], ones1[:],
                                       tmpA[:, 3 * g:3 * g + 3],
                                       start=(g == 0), stop=(g == 7)))
            return outs
        s_hsums = sch.step("tensor", hsums_fn, deps=[s_hsq, s_consts, s_ones],
                           small=True)

        def hrows_fn(eng):
            outs = []
            outs.append(eng.tensor_scalar(out=rowa[:, 0:3], in0=ps_s1[0:1, 0:3],
                                          scalar1=1.0 / DM, scalar2=None, op0=ALU.mult))
            outs.append(eng.tensor_scalar(out=rowb[:, 0:3], in0=ps_s2[0:1, 0:3],
                                          scalar1=1.0 / DM, scalar2=None, op0=ALU.mult))
            eng.drain()
            outs.append(eng.tensor_tensor(out=rowc[:, 0:3], in0=rowa[:, 0:3],
                                          in1=rowa[:, 0:3], op=ALU.mult))
            eng.drain()
            outs.append(eng.tensor_tensor(out=rowb[:, 0:3], in0=rowb[:, 0:3],
                                          in1=rowc[:, 0:3], op=ALU.subtract))
            return outs
        s_hrows = sch.step("vector", hrows_fn, deps=[s_hsums], small=True)

        def hrstd_fn(eng):
            o1 = eng.activation(out=rowc[:, 0:3], in_=rowb[:, 0:3], func=AF.Ln,
                                bias=cbank[0:1, 0:1])
            eng.drain()
            o2 = eng.activation(out=rowb[:, 0:3], in_=rowc[:, 0:3], func=AF.Exp,
                                scale=cbank[0:1, 1:2])
            return [o1, o2]
        s_hrstd = sch.step("scalar", hrstd_fn, deps=[s_hrows, s_ones], small=True)

        def hmursd_fn(eng):
            return [eng.tensor_tensor(out=rowc[:, 0:3], in0=rowa[:, 0:3],
                                      in1=rowb[:, 0:3], op=ALU.mult)]
        s_hmursd = sch.step("vector", hmursd_fn, deps=[s_hrstd], small=True)

        def hbc_fn(eng):
            o1 = eng.matmul(ps_r[:, 0:3], onesr[:], rowb[:, 0:3], start=True, stop=True)
            o2 = eng.matmul(ps_m[:, 0:3], onesr[:], rowc[:, 0:3], start=True, stop=True)
            return [o1, o2]
        s_hbc = sch.step("tensor", hbc_fn, deps=[s_hmursd])

        def hxn_fn(eng):
            outs = []
            for g in range(8):
                outs.append(eng.tensor_tensor(out=tmpA[:, 0:3], in0=xh[:, 3 * g:3 * g + 3],
                                              in1=ps_r[:, 0:3], op=ALU.mult))
                eng.drain()
                outs.append(eng.tensor_tensor(out=xh[:, 3 * g:3 * g + 3], in0=tmpA[:, 0:3],
                                              in1=ps_m[:, 0:3], op=ALU.subtract))
                eng.drain()
            return outs
        s_hxn = sch.step("vector", hxn_fn, deps=[s_hbc, s_hsums], small=True)

        def hsq2_fn(eng):
            return [eng.activation(out=tmpA[:, 0:24], in_=xh[:], func=AF.Square)]
        s_hsq2 = sch.step("scalar", hsq2_fn, deps=[s_hxn])

        def hsums2_fn(eng):
            outs = []
            for g in range(8):
                outs.append(eng.matmul(ps_s1[0:1, 0:3], ones1[:],
                                       xh[:, 3 * g:3 * g + 3],
                                       start=(g == 0), stop=(g == 7)))
            for g in range(8):
                outs.append(eng.matmul(ps_s2[0:1, 0:3], ones1[:],
                                       tmpA[:, 3 * g:3 * g + 3],
                                       start=(g == 0), stop=(g == 7)))
            return outs
        s_hsums2 = sch.step("tensor", hsums2_fn, deps=[s_hsq2, s_hxn, s_hrows],
                            small=True)

        s_hrows2 = sch.step("vector", hrows_fn, deps=[s_hsums2], small=True)
        s_hrstd2 = sch.step("scalar", hrstd_fn, deps=[s_hrows2], small=True)
        s_hmursd2 = sch.step("vector", hmursd_fn, deps=[s_hrstd2], small=True)
        s_hbc2 = sch.step("tensor", hbc_fn, deps=[s_hmursd2, s_hxn], small=True)

        def hXn_fn(eng):
            outs = []
            for g in range(8):
                outs.append(eng.tensor_tensor(out=tmpA[:, 0:3], in0=xh[:, 3 * g:3 * g + 3],
                                              in1=ps_r[:, 0:3], op=ALU.mult))
                eng.drain()
                outs.append(eng.tensor_tensor(out=tmpB[:, 0:3], in0=tmpA[:, 0:3],
                                              in1=ps_m[:, 0:3], op=ALU.subtract))
                eng.drain()
                outs.append(eng.tensor_scalar(out=tmpA[:, 0:3], in0=tmpB[:, 0:3],
                                              scalar1=lng[:, g:g + 1], scalar2=lnb[:, g:g + 1],
                                              op0=ALU.mult, op1=ALU.add))
                eng.drain()
                outs.append(eng.tensor_tensor(out=xnhb[:, 3 * g:3 * g + 3], in0=tmpA[:, 0:3],
                                              in1=hm[:], op=ALU.mult))
                eng.drain()
            return outs
        s_hXn = sch.step("vector", hXn_fn, deps=[s_hbc2], small=True)

        hx_steps = []
        for g in range(16):
            def hxm_fn(eng, g=g):
                outs = []
                for k in range(8):
                    outs.append(eng.matmul(ps_xm[g % 2][:, 0:3],
                                           wu1[k][:, 128 * g:128 * (g + 1)],
                                           xnhb[:, 3 * k:3 * k + 3],
                                           start=(k == 0), stop=(k == 7)))
                return outs
            deps = [s_hXn] + cvt_steps[:8] + ([hx_steps[-2][1]] if g >= 2 else [])
            s_mm = sch.step("tensor", hxm_fn, deps=deps, small=True)

            def hev_fn(eng, g=g):
                return [eng.copy(out=hsave[:, 3 * g:3 * g + 3], in_=ps_xm[g % 2][:, 0:3])]
            s_ev = sch.step("scalar", hev_fn, deps=[s_mm], small=True)
            hx_steps.append((s_mm, s_ev))

        prev_chunk_consumers = []
        prev_xn_out = None
        prev_conv = {}
        prev_silu = {}
        prev_xst = {}
        prev_pp_chain = {}
        prev_pp_consumers = []
        store_steps = []
        for c in range(NCH):
            @dmas(8)
            def xload_fn(eng, c=c):
                return [eng.dma_start(out=xc[g][:],
                                      in_=X[128 * g:128 * (g + 1), 3 + c * T:3 + (c + 1) * T])
                        for g in range(8)]
            s_xload = sch.dma("sync", xload_fn, deps=prev_chunk_consumers)

            s_bc1, s_sq1 = emit_ln(sch, P, xc, 8, T, [s_xload],
                                   dep_sums=([s_hrows2] if c == 0 else []))

            def xn_fn(eng):
                outs = []
                for g in range(8):
                    outs.append(eng.tensor_tensor(out=tmpA[:], in0=xc[g][:],
                                                  in1=ps_r[:], op=ALU.mult))
                    outs.append(eng.tensor_tensor(out=xc[g][:], in0=tmpA[:],
                                                  in1=ps_m[:], op=ALU.subtract))
                return outs
            s_xn = sch.step("vector", xn_fn, deps=[s_bc1, s_sq1])

            s_bc2, s_sq2 = emit_ln(sch, P, xc, 8, T, [s_xn])

            def Xn_fn(eng):
                outs = []
                for g in range(8):
                    outs.append(eng.tensor_tensor(out=tmpA[:], in0=xc[g][:],
                                                  in1=ps_r[:], op=ALU.mult))
                    outs.append(eng.tensor_tensor(out=tmpB[:], in0=tmpA[:],
                                                  in1=ps_m[:], op=ALU.subtract))
                    outs.append(eng.tensor_scalar(out=xnbf[g][:], in0=tmpB[:],
                                                  scalar1=lng[:, g:g + 1],
                                                  scalar2=lnb[:, g:g + 1],
                                                  op0=ALU.mult, op1=ALU.add))
                return outs
            dep_xn_war = [prev_xn_out] if prev_xn_out is not None else []
            s_Xn = sch.step("vector", Xn_fn, deps=[s_bc2, s_sq2] + dep_xn_war)
            prev_chunk_consumers = [s_Xn, s_sq2]

            @dmas(8)
            def xnstore_fn(eng, c=c):
                return [eng.dma_start(out=O_XN[128 * g:128 * (g + 1), c * T:(c + 1) * T],
                                      in_=xnbf[g][:]) for g in range(8)]
            s_xnstore = sch.dma("gpsimd", xnstore_fn, deps=[s_Xn])
            prev_xn_out = s_xnstore
            store_steps.append(s_xnstore)

            xm_mm_prev = None
            for g in range(16):
                def xm_fn(eng, g=g):
                    outs = []
                    for k in range(8):
                        outs.append(eng.matmul(ps_xm[g % 2][:],
                                               wu1[k][:, 128 * g:128 * (g + 1)],
                                               xnbf[k][:], start=(k == 0), stop=(k == 7)))
                    return outs
                deps = [s_Xn]
                if c == 0 and g < 2:
                    deps.append(hx_steps[14 + g][1])
                if (c, g - 2) in prev_conv or (c - 1, g + 14) in prev_conv:
                    pass
                key = (c, g - 2) if g >= 2 else (c - 1, 14 + g)
                if key in prev_silu:
                    deps.append(prev_silu[key][0])
                s_mm = sch.step("tensor", xm_fn, deps=deps)

                def evac_fn(eng, g=g):
                    return [eng.copy(out=convb[:, 3:T + 3], in_=ps_xm[g % 2][:])]
                dep_e = [s_mm]
                pk = (c, g - 1) if g >= 1 else (c - 1, 15)
                if pk in prev_conv:
                    dep_e.append(prev_conv[pk])
                if c == 0 and g == 0:
                    dep_e.append(hx_steps[15][1])
                s_evac = sch.step("scalar", evac_fn, deps=dep_e, small=True)

                def conv_fn(eng, g=g):
                    outs = []
                    outs.append(eng.tensor_copy(out=convb[:, 0:3],
                                                in_=hsave[:, 3 * g:3 * g + 3]))
                    outs.append(eng.tensor_copy(out=hsave[:, 3 * g:3 * g + 3],
                                                in_=convb[:, T:T + 3]))
                    eng.drain()
                    for k in range(4):
                        outs.append(eng.tensor_scalar(out=ct[k][:], in0=convb[:, k:k + T],
                                                      scalar1=cw[:, 4 * g + k:4 * g + k + 1],
                                                      scalar2=None, op0=ALU.mult))
                    outs.append(eng.tensor_tensor(out=cs[0][:], in0=ct[0][:], in1=ct[1][:],
                                                  op=ALU.add))
                    outs.append(eng.tensor_tensor(out=cs[1][:], in0=ct[2][:], in1=ct[3][:],
                                                  op=ALU.add))
                    outs.append(eng.tensor_tensor(out=cs[0][:], in0=cs[0][:], in1=cs[1][:],
                                                  op=ALU.add))
                    return outs
                dep_c = [s_evac]
                pk2 = (c, g - 1) if g >= 1 else (c - 1, 15)
                if pk2 in prev_silu:
                    dep_c.append(prev_silu[pk2][1])
                s_conv = sch.step("vector", conv_fn, deps=dep_c)
                prev_conv[(c, g)] = s_conv

                def silu_fn(eng, g=g):
                    return [eng.activation(out=xms[g][:], in_=cs[0][:], func=AF.Silu,
                                           bias=cb[:, g:g + 1])]
                dep_s = [s_conv]
                if prev_pp_consumers:
                    dep_s.append(prev_pp_consumers[-1])
                if (c - 1, g) in prev_xst:
                    dep_s.append(prev_xst[(c - 1, g)])
                s_silu = sch.step("scalar", silu_fn, deps=dep_s)
                prev_silu[(c, g)] = (s_evac, s_silu)

                @dmas(1)
                def xms_store_fn(eng, c=c, g=g):
                    return [eng.dma_start(
                        out=O_XMS[128 * g:128 * (g + 1), c * T:(c + 1) * T],
                        in_=xms[g][:])]
                s_xst = sch.dma("gpsimd", xms_store_fn, deps=[s_silu])
                prev_xst[(c, g)] = s_xst
                store_steps.append(s_xst)

            silu_all = [prev_silu[(c, g)][1] for g in range(16)]

            pp_last_mm = None
            pp_dl_steps = []
            for t in range(17):
                def pp_fn(eng, t=t):
                    outs = []
                    w = 128 if t < 16 else 32
                    for k in range(16):
                        outs.append(eng.matmul(
                            ps_pp[t % 2][0:w, :],
                            wll[k][:, 128 * t:128 * t + w],
                            xms[k][:], start=(k == 0), stop=(k == 15)))
                    return outs
                deps = silu_all + cvt_steps[8:]
                if t >= 2:
                    deps.append(pp_dl_steps[t - 2][0])
                elif (c - 1, 15 + t) in prev_pp_chain:
                    deps.append(prev_pp_chain[(c - 1, 15 + t)])
                s_ppmm = sch.step("tensor", pp_fn, deps=deps)

                if t < 16:
                    def sp1_fn(eng, t=t):
                        o1 = eng.tensor_scalar(out=zz[:], in0=ps_pp[t % 2][:],
                                               scalar1=bll[:, t:t + 1], scalar2=None,
                                               op0=ALU.add)
                        o2 = eng.tensor_scalar(out=mz[:], in0=zz[:], scalar1=0.0,
                                               scalar2=None, op0=ALU.max)
                        return [o1, o2]
                    s_sp1 = sch.step("vector", sp1_fn, deps=[s_ppmm])

                    def sp2_fn(eng, t=t):
                        o0 = eng.activation(out=az[:], in_=zz[:], func=AF.Abs)
                        o1 = eng.activation(out=az[:], in_=az[:], func=AF.Exp, scale=-1.0)
                        o2 = eng.activation(out=lz[:], in_=az[:], func=AF.Ln,
                                            bias=cbank[:, 2:3])
                        return [o0, o1, o2]
                    s_sp2 = sch.step("scalar", sp2_fn, deps=[s_sp1])

                    def sp3_fn(eng, t=t):
                        return [eng.tensor_tensor(out=dl[t % 2][:], in0=mz[:], in1=lz[:],
                                                  op=ALU.add)]
                    dep3 = [s_sp2]
                    if t >= 2:
                        dep3.append(pp_dl_steps[t - 2][1])
                    s_sp3 = sch.step("vector", sp3_fn, deps=dep3)

                    @dmas(1)
                    def dst_fn(eng, c=c, t=t):
                        return [eng.dma_start(
                            out=O_DELTA[128 * t:128 * (t + 1), c * T:(c + 1) * T],
                            in_=dl[t % 2][:])]
                    s_dst = sch.dma("gpsimd", dst_fn, deps=[s_sp3])
                    pp_dl_steps.append((s_sp1, s_dst))
                    if t == 15:
                        prev_pp_chain[(c, 16)] = s_sp1
                    store_steps.append(s_dst)
                else:
                    def bc_fn2(eng, t=t):
                        return [eng.tensor_scalar(out=bcb[:], in0=ps_pp[t % 2][0:32, :],
                                                  scalar1=bll[0:32, 16:17], scalar2=None,
                                                  op0=ALU.add)]
                    s_bcr = sch.step("vector", bc_fn2, deps=[s_ppmm], small=True)

                    @dmas(1)
                    def bcst_fn(eng, c=c):
                        return [eng.dma_start(out=O_BC[:, c * T:(c + 1) * T], in_=bcb[:])]
                    s_bcst = sch.dma("gpsimd", bcst_fn, deps=[s_bcr])
                    store_steps.append(s_bcst)
                    prev_pp_chain[(c, 15)] = s_bcr
                    prev_pp_consumers = [s_ppmm]
                pp_last_mm = s_ppmm

        @dmas(2)
        def dbg_fn(eng):
            return [eng.dma_start(out=O_HS, in_=hsave[:]),
                    eng.dma_start(out=O_XNHB, in_=xnhb[:])]
        s_dbg = sch.dma("gpsimd", dbg_fn, deps=[hx_steps[15][1], s_hXn])
        store_steps.append(s_dbg)

        def fin_fn(eng):
            return [eng.memset(rowa[0:1, 0:1], 0.0)]
        sch.step("vector", fin_fn, deps=store_steps)

        sch.emit(block)
    return nc


def build_L2():
    T = 512
    NCH = LTOT // T
    NG = DSL // 128
    nc = bass.Bass("TRN2", target_bir_lowering=False)
    DELTA = nc.dram_tensor("DELTA", [DSL, LTOT], F32, kind="ExternalInput").ap()
    XMS = nc.dram_tensor("XMS", [DSL, LTOT], BF16, kind="ExternalInput").ap()
    BFI = nc.dram_tensor("BFI", [NST, LTOT], BF16, kind="ExternalInput").ap()
    CFI = nc.dram_tensor("CFI", [NST, LTOT], BF16, kind="ExternalInput").ap()
    XNF = nc.dram_tensor("XNF", [DM, LTOT], BF16, kind="ExternalInput").ap()
    ALOG = nc.dram_tensor("ALOG", [DSL, NST], F32, kind="ExternalInput").ap()
    WU2T = nc.dram_tensor("WU2T", [DM, DSL], F32, kind="ExternalInput").ap()
    WDT = nc.dram_tensor("WDT", [DSL, DM], F32, kind="ExternalInput").ap()
    O_YP = nc.dram_tensor("O_YP", [DM, LTOT], F32, kind="ExternalOutput").ap()

    NT = NST * T

    with ExitStack() as ctx:
        e = ctx.enter_context
        wu2 = [e(nc.sbuf_tensor(f"wu2_{k}", [128, DSL], BF16)) for k in range(8)]
        wdt = [e(nc.sbuf_tensor(f"wdt_{g}", [128, DM], BF16)) for g in range(NG)]
        wtmp = [e(nc.sbuf_tensor(f"wtmp{i}", [128, DM // 2], F32)) for i in range(2)]
        alog = [e(nc.sbuf_tensor(f"alog{g}", [128, NST], F32)) for g in range(NG)]
        Aa = [e(nc.sbuf_tensor(f"Aa{g}", [128, NST], F32)) for g in range(NG)]
        iA = [e(nc.sbuf_tensor(f"iA{g}", [128, NST], F32)) for g in range(NG)]
        eAn = e(nc.sbuf_tensor("eAn", [128, NST], F32))
        hst = [e(nc.sbuf_tensor(f"hst{g}", [128, NST], F32)) for g in range(NG)]
        delta = [e(nc.sbuf_tensor(f"delta_{g}", [128, T], F32)) for g in range(NG)]
        xmsb = [[e(nc.sbuf_tensor(f"xms{p}_{g}", [128, T], BF16)) for g in range(NG)]
                for p in range(2)]
        xnc = [e(nc.sbuf_tensor(f"xnc_{k}", [128, T], BF16)) for k in range(8)]
        bbc = [e(nc.sbuf_tensor(f"bbc{i}", [128, NT], BF16)) for i in range(2)]
        cbc = e(nc.sbuf_tensor("cbc", [128, NT], BF16))
        abar = e(nc.sbuf_tensor("abar", [128, NT], F32))
        am1 = e(nc.sbuf_tensor("am1", [128, NT], BF16))
        uu = e(nc.sbuf_tensor("uu", [128, NT], BF16))
        hh = [e(nc.sbuf_tensor(f"hh{g}", [128, NT], BF16)) for g in range(NG)]
        qq = e(nc.sbuf_tensor("qq", [128, NT], BF16))
        t2 = e(nc.sbuf_tensor("t2", [128, NT // 4], BF16))
        ybf = [e(nc.sbuf_tensor(f"ybf{g}", [128, T], BF16)) for g in range(NG)]
        xg = [e(nc.sbuf_tensor(f"xg{g}", [128, T], BF16)) for g in range(NG)]
        Yb = [[e(nc.sbuf_tensor(f"Yb{p}_{g}", [128, T], BF16)) for g in range(NG)]
              for p in range(2)]

        evac = [e(nc.sbuf_tensor(f"evac{m}", [128, T], F32)) for m in range(2)]
        ps_xg = [e(nc.psum_tensor(f"ps_xg{g}", [128, T], F32)) for g in range(NG)]
        ps_yp = [e(nc.psum_tensor(f"ps_yp{i}", [128, T], F32)) for i in range(2)]

        sch = Sched(nc, ctx)
        block = e(nc.Block())

        @dmas(2)
        def alog_fn(eng):
            return [eng.dma_start(out=alog[g][:], in_=ALOG[128 * g:128 * (g + 1), :])
                    for g in range(NG)]
        s_alog = sch.dma("sync", alog_fn)

        aprep_steps = []
        for g in range(NG):
            def aexp_fn(eng, g=g):
                o1 = eng.activation(out=Aa[g][:], in_=alog[g][:], func=AF.Exp)
                o2 = eng.activation(out=eAn[:], in_=alog[g][:], func=AF.Exp, scale=-1.0)
                return [o1, o2]
            s_ae = sch.step("scalar", aexp_fn,
                            deps=[s_alog] + ([aprep_steps[-1]] if aprep_steps else []),
                            small=True)

            def aneg_fn(eng, g=g):
                o1 = eng.tensor_scalar(out=Aa[g][:], in0=Aa[g][:], scalar1=-1.0,
                                       scalar2=None, op0=ALU.mult)
                o2 = eng.tensor_scalar(out=iA[g][:], in0=eAn[:], scalar1=-1.0,
                                       scalar2=None, op0=ALU.mult)
                o3 = eng.memset(hst[g][:], 0.0)
                return [o1, o2, o3]
            s_an = sch.step("vector", aneg_fn, deps=[s_ae], small=True)
            aprep_steps.append(s_an)

        cvt = []
        i = 0
        for k in range(10):
            dst = wu2[k] if k < 8 else wdt[k - 8]
            w = DSL if k < 8 else DM
            SRC = WU2T if k < 8 else WDT
            row = 128 * k if k < 8 else 128 * (k - 8)
            for h in range(2):
                w2 = w // 2

                @dmas(1)
                def wl_fn(eng, SRC=SRC, row=row, h=h, w2=w2, i=i):
                    return [eng.dma_start(out=wtmp[i % 2][:, 0:w2],
                                          in_=SRC[row:row + 128, h * w2:(h + 1) * w2])]
                sw = sch.dma("sync", wl_fn, deps=[cvt[i - 2]] if i >= 2 else [])

                def cv_fn(eng, dst=dst, i=i, h=h, w2=w2):
                    return [eng.tensor_copy(out=dst[:, h * w2:(h + 1) * w2],
                                            in_=wtmp[i % 2][:, 0:w2])]
                sc = sch.step("vector", cv_fn, deps=[sw])
                cvt.append(sc)
                i += 1

        prev = {}
        store_steps = []
        for c in range(NCH):
            p = c % 2

            @dmas(2 * NG)
            def load_fn(eng, c=c, p=p):
                outs = []
                for g in range(NG):
                    outs.append(eng.dma_start(
                        out=delta[g][:],
                        in_=DELTA[128 * g:128 * (g + 1), c * T:(c + 1) * T]))
                    outs.append(eng.dma_start(
                        out=xmsb[p][g][:],
                        in_=XMS[128 * g:128 * (g + 1), c * T:(c + 1) * T]))
                return outs
            dep_l = []
            if c >= 1:
                dep_l = [prev[(c - 1, 'abarB', NG - 1)]]
            if c >= 2:
                dep_l.append(prev[(c - 2, 'scan', NG - 1)])
            s_load = sch.dma("sync", load_fn, deps=dep_l)

            @dmas(8)
            def xnload_fn(eng, c=c):
                return [eng.dma_start(
                    out=xnc[k][:],
                    in_=XNF[128 * k:128 * (k + 1), c * T:(c + 1) * T])
                    for k in range(8)]
            dep_xl = [prev[(c - 1, 'xg', NG - 1)]] if c >= 1 else []
            s_xnload = sch.dma("sync", xnload_fn, deps=dep_xl)

            @dmas(1)
            def bcb_fn(eng, c=c, p=p):
                return [eng.dma_start(
                    out=bbc[p].ap().rearrange("p (n t) -> p n t", n=NST),
                    in_=BFI[:, c * T:(c + 1) * T].unsqueeze(0).broadcast_to([128, NST, T]))]
            dep_bb = [prev[(c - 2, 'xqq', NG - 1)]] if c >= 2 else []
            s_bcb = sch.dma("sync", bcb_fn, deps=dep_bb)

            @dmas(1)
            def bcc_fn(eng, c=c):
                return [eng.dma_start(
                    out=cbc.ap().rearrange("p (n t) -> p n t", n=NST),
                    in_=CFI[:, c * T:(c + 1) * T].unsqueeze(0).broadcast_to([128, NST, T]))]
            dep_bc = [prev[(c - 1, 'ytail', NG - 1)]] if c >= 1 else []
            s_bcc = sch.dma("sync", bcc_fn, deps=dep_bc)

            for g in range(NG):
                NH2 = NST // 2

                def xqq_fn(eng, c=c, g=g, p=p):
                    return [eng.tensor_tensor(
                        out=qq.ap().rearrange("p (n t) -> p n t", n=NST),
                        in0=xmsb[p][g].ap().unsqueeze(1).broadcast_to([128, NST, T]),
                        in1=bbc[p].ap().rearrange("p (n t) -> p n t", n=NST),
                        op=ALU.mult)]
                dep_x0 = [s_load, s_bcb]
                if (c, 'ytail', g - 1) in prev:
                    dep_x0.append(prev[(c, 'ytail', g - 1)])
                elif (c - 1, 'ytail', NG - 1) in prev:
                    dep_x0.append(prev[(c - 1, 'ytail', NG - 1)])
                s_xqq = sch.step("vector", xqq_fn, deps=dep_x0)
                prev[(c, 'xqq', g)] = s_xqq

                half_steps = []
                for hf in range(2):
                    n0 = hf * NH2

                    def abar_fn(eng, c=c, g=g, p=p, n0=n0):
                        outs = []
                        for n in range(n0, n0 + NH2):
                            outs.append(eng.activation(
                                out=abar[:, n * T:(n + 1) * T], in_=delta[g][:],
                                func=AF.Exp, scale=Aa[g][:, n:n + 1]))
                        return outs
                    dep_a = [s_load, aprep_steps[g]]
                    key = (c, 'scan' + 'AB'[hf], g - 1)
                    key2 = (c - 1, 'scan' + 'AB'[hf], NG - 1)
                    if key in prev:
                        dep_a.append(prev[key])
                    elif key2 in prev:
                        dep_a.append(prev[key2])
                    s_abar = sch.step("scalar", abar_fn, deps=dep_a)
                    prev[(c, 'abar' + 'AB'[hf], g)] = s_abar

                    def scanh_fn(eng, c=c, g=g, p=p, n0=n0):
                        outs = []
                        for n in range(n0, n0 + NH2):
                            outs.append(eng.tensor_scalar(
                                out=am1[:, n * T:(n + 1) * T],
                                in0=abar[:, n * T:(n + 1) * T],
                                scalar1=iA[g][:, n:n + 1], scalar2=iA[g][:, n:n + 1],
                                op0=ALU.mult, op1=ALU.subtract))
                        lo, hi = n0 * T, (n0 + NH2) * T
                        outs.append(eng.tensor_tensor(out=uu[:, lo:hi],
                                                      in0=am1[:, lo:hi],
                                                      in1=qq[:, lo:hi], op=ALU.mult))
                        for n in range(n0, n0 + NH2):
                            init = (hst[g][:, n:n + 1] if c == 0
                                    else hh[g][:, (n + 1) * T - 1:(n + 1) * T])
                            outs.append(eng.tensor_tensor_scan(
                                out=hh[g][:, n * T:(n + 1) * T],
                                data0=abar[:, n * T:(n + 1) * T],
                                data1=uu[:, n * T:(n + 1) * T],
                                initial=init,
                                op0=ALU.mult, op1=ALU.add))
                        return outs
                    s_scanh = sch.step("vector", scanh_fn, deps=[s_abar, s_xqq])
                    prev[(c, 'scan' + 'AB'[hf], g)] = s_scanh
                    half_steps.append(s_scanh)

                def ytail_fn(eng, c=c, g=g):
                    outs = []
                    outs.append(eng.tensor_tensor(out=qq[:], in0=hh[g][:], in1=cbc[:],
                                                  op=ALU.mult))
                    outs.append(eng.tensor_tensor(out=uu[:, 0:NT // 2], in0=qq[:, 0:NT // 2],
                                                  in1=qq[:, NT // 2:], op=ALU.add))
                    outs.append(eng.tensor_tensor(out=t2[:], in0=uu[:, 0:NT // 4],
                                                  in1=uu[:, NT // 4:NT // 2], op=ALU.add))
                    outs.append(eng.tensor_tensor(out=uu[:, 0:NT // 8], in0=t2[:, 0:NT // 8],
                                                  in1=t2[:, NT // 8:], op=ALU.add))
                    outs.append(eng.tensor_tensor(out=ybf[g][:], in0=uu[:, 0:T],
                                                  in1=uu[:, T:NT // 8], op=ALU.add))
                    return outs
                s_yt = sch.step("vector", ytail_fn, deps=half_steps + [s_bcc])
                prev[(c, 'scan', g)] = s_yt
                prev[(c, 'ytail', g)] = s_yt

                def xg_fn(eng, c=c, g=g, p=p):
                    outs = []
                    for k in range(8):
                        outs.append(eng.matmul(ps_xg[g][:],
                                               wu2[k][:, 128 * g:128 * (g + 1)],
                                               xnc[k][:], start=(k == 0), stop=(k == 7)))
                    return outs
                dep_xg = [s_xnload] + cvt[:16]
                if (c - 1, 'silu', g) in prev:
                    dep_xg.append(prev[(c - 1, 'silu', g)])
                s_xg = sch.step("tensor", xg_fn, deps=dep_xg)
                prev[(c, 'xg', g)] = s_xg

                def silu_fn(eng, c=c, g=g):
                    return [eng.activation(out=xg[g][:], in_=ps_xg[g][:], func=AF.Silu)]
                dep_si = [s_xg]
                if (c - 1, 'Y') in prev:
                    dep_si.append(prev[(c - 1, 'Y')])
                s_silu = sch.step("scalar", silu_fn, deps=dep_si)
                prev[(c, 'silu', g)] = s_silu

            def Y_fn(eng, c=c, p=p):
                outs = []
                for g in range(NG):
                    outs.append(eng.tensor_tensor(out=Yb[p][g][:], in0=ybf[g][:],
                                                  in1=xg[g][:], op=ALU.mult))
                return outs
            dep_Y = [prev[(c, 'scan', g)] for g in range(NG)]
            dep_Y += [prev[(c, 'silu', g)] for g in range(NG)]
            if c >= 2:
                dep_Y.append(prev[(c - 2, 'yp', 7)])
            s_Y = sch.step("vector", Y_fn, deps=dep_Y)
            prev[(c, 'Y')] = s_Y

            for cc in ([c - 1] if c >= 1 else []) + ([c] if c == NCH - 1 else []):
                pp2 = cc % 2
                for m in range(8):
                    def yp_fn(eng, cc=cc, m=m, pp2=pp2):
                        outs = []
                        for g in range(NG):
                            outs.append(eng.matmul(ps_yp[m % 2][:],
                                                   wdt[g][:, 128 * m:128 * (m + 1)],
                                                   Yb[pp2][g][:], start=(g == 0),
                                                   stop=(g == NG - 1)))
                        return outs
                    dep_yp = [prev[(cc, 'Y')]] + cvt[16:]
                    if (cc, 'evac', m - 2) in prev:
                        dep_yp.append(prev[(cc, 'evac', m - 2)])
                    elif cc >= 1 and m < 2:
                        dep_yp.append(prev[(cc - 1, 'evac', m + 6)])
                    s_yp = sch.step("tensor", yp_fn, deps=dep_yp)
                    prev[(cc, 'yp', m)] = s_yp

                    def ev_fn(eng, cc=cc, m=m):
                        return [eng.copy(out=evac[m % 2][:], in_=ps_yp[m % 2][:])]
                    dep_ev = [s_yp]
                    if (cc, 'store', m - 2) in prev:
                        dep_ev.append(prev[(cc, 'store', m - 2)])
                    elif cc >= 1 and m < 2:
                        dep_ev.append(prev[(cc - 1, 'store', m + 6)])
                    s_ev = sch.step("scalar", ev_fn, deps=dep_ev)
                    prev[(cc, 'evac', m)] = s_ev

                    @dmas(1)
                    def st_fn(eng, cc=cc, m=m):
                        return [eng.dma_start(
                            out=O_YP[128 * m:128 * (m + 1), cc * T:(cc + 1) * T],
                            in_=evac[m % 2][:])]
                    s_st = sch.dma("gpsimd", st_fn, deps=[s_ev])
                    prev[(cc, 'store', m)] = s_st
                    store_steps.append(s_st)

        def fin_fn(eng):
            return [eng.memset(eAn[0:1, 0:1], 0.0)]
        sch.step("vector", fin_fn, deps=store_steps)

        sch.emit(block)
    return nc


def build_L3():
    T = 256
    NCH = TC // T
    HID = 3072
    NH = HID // 128
    nc = bass.Bass("TRN2", target_bir_lowering=False)
    X = nc.dram_tensor("X", [DM, TC], F32, kind="ExternalInput").ap()
    YP = nc.dram_tensor("YP", [NCORE * DM, TC], F32, kind="ExternalInput").ap()
    BD = nc.dram_tensor("BD", [128, 8], F32, kind="ExternalInput").ap()
    WGT = nc.dram_tensor("WGT", [DM, HID], I32, kind="ExternalInput").ap()
    WUT = nc.dram_tensor("WUT", [DM, HID], I32, kind="ExternalInput").ap()
    WDT3 = nc.dram_tensor("WDT3", [HID, DM], I32, kind="ExternalInput").ap()
    O_OUT = nc.dram_tensor("O_OUT", [DM, TC], F32, kind="ExternalOutput").ap()

    with ExitStack() as ctx:
        e = ctx.enter_context
        wg = [e(nc.sbuf_tensor(f"wg{k}", [128, HID], BF16)) for k in range(8)]
        wu = [e(nc.sbuf_tensor(f"wu{k}", [128, HID], BF16)) for k in range(8)]
        wd = [e(nc.sbuf_tensor(f"wd{k}", [128, DM], BF16)) for k in range(NH)]
        wtmp = [e(nc.sbuf_tensor(f"wtmp{i}", [128, HID // 2], I32)) for i in range(2)]
        bd = e(nc.sbuf_tensor("bd", [128, 8], F32))
        ones1 = e(nc.sbuf_tensor("ones1", [128, 1], F32))
        onesr = e(nc.sbuf_tensor("onesr", [1, 128], F32))
        cbank = e(nc.sbuf_tensor("cbank", [128, 4], F32))
        xc = [e(nc.sbuf_tensor(f"xc{g}", [128, T], F32)) for g in range(8)]
        xn = [e(nc.sbuf_tensor(f"xn{g}", [128, T], F32)) for g in range(8)]
        xnb = [e(nc.sbuf_tensor(f"xnb{g}", [128, T], BF16)) for g in range(8)]
        sq = xn
        tmpA = e(nc.sbuf_tensor("tmpA", [128, T], F32))
        rowa = e(nc.sbuf_tensor("rowa", [1, T], F32))
        rowb = e(nc.sbuf_tensor("rowb", [1, T], F32))
        rowc = e(nc.sbuf_tensor("rowc", [1, T], F32))
        gsil = e(nc.sbuf_tensor("gsil", [128, T], BF16))
        hply = [e(nc.sbuf_tensor(f"hply{k}", [128, T], BF16)) for k in range(NH)]
        ypt = [e(nc.sbuf_tensor(f"ypt{i}", [128, T], F32)) for i in range(8)]
        acc = e(nc.sbuf_tensor("acc", [128, T], F32))
        outt = [e(nc.sbuf_tensor(f"outt{i}", [128, T], F32)) for i in range(2)]

        ps_s1 = e(nc.psum_tensor("ps_s1", [1, T], F32))
        ps_s2 = e(nc.psum_tensor("ps_s2", [1, T], F32))
        ps_r = e(nc.psum_tensor("ps_r", [128, T], F32))
        ps_m = e(nc.psum_tensor("ps_m", [128, T], F32))
        ps_g = e(nc.psum_tensor("ps_g", [128, T], F32))
        ps_u = e(nc.psum_tensor("ps_u", [128, T], F32))
        ps_d = [e(nc.psum_tensor(f"ps_d{i}", [128, T], F32)) for i in range(2)]

        sch = Sched(nc, ctx)
        block = e(nc.Block())
        def ones_fn(eng):
            return [eng.memset(ones1[:], 1.0), eng.memset(onesr[:], 1.0),
                    eng.memset(cbank[:, 0:1], float(EPS)),
                    eng.memset(cbank[:, 1:2], -0.5)]
        s_ones = sch.step("vector", ones_fn, small=True)
        P = dict(sq=sq, rowa=rowa, rowb=rowb, rowc=rowc, ones1=ones1,
                 onesr=onesr, ps_s1=ps_s1, ps_s2=ps_s2, ps_r=ps_r, ps_m=ps_m,
                 cb_eps=cbank[0:1, 0:1], cb_mhalf=cbank[0:1, 1:2],
                 cb_dep=[s_ones])

        @dmas(1)
        def bd_fn(eng):
            return [eng.dma_start(out=bd[:], in_=BD)]
        s_bd = sch.dma("sync", bd_fn)

        cvt = []
        srcs = ([(WGT, wg[k], HID, k) for k in range(8)]
                + [(WUT, wu[k], HID, k) for k in range(8)]
                + [(WDT3, wd[k], DM, k) for k in range(NH)])
        i = 0
        for (SRC, dst, w, k) in srcs:
            for h in range(2):
                w2 = w // 2

                @dmas(1)
                def wl_fn(eng, SRC=SRC, k=k, w2=w2, h=h, i=i):
                    return [eng.dma_start(
                        out=wtmp[i % 2][:, 0:w2],
                        in_=SRC[128 * k:128 * (k + 1), h * w2:(h + 1) * w2])]
                sw = sch.dma("sync", wl_fn, deps=[cvt[i - 2]] if i >= 2 else [])

                def cv_fn(eng, dst=dst, i=i, w2=w2, h=h):
                    return [eng.tensor_scalar(
                        out=dst[:, h * w2:(h + 1) * w2], in0=wtmp[i % 2][:, 0:w2],
                        scalar1=float(1.0 / 127.0), scalar2=None, op0=ALU.mult)]
                sc = sch.step("vector", cv_fn, deps=[sw])
                cvt.append(sc)
                i += 1

        prev = {}
        store_steps = []
        for c in range(NCH):
            @dmas(8)
            def xl_fn(eng, c=c):
                return [eng.dma_start(out=xc[g][:],
                                      in_=X[128 * g:128 * (g + 1), c * T:(c + 1) * T])
                        for g in range(8)]
            dep_x = [prev[(c - 1, 'fin', 7)]] if c >= 1 else []
            s_xl = sch.dma("sync", xl_fn, deps=dep_x)

            s_bc1, s_sq1 = emit_ln(sch, P, xc, 8, T, [s_xl])

            def xn_fn(eng, c=c):
                outs = []
                for g in range(8):
                    outs.append(eng.tensor_tensor(out=tmpA[:], in0=xc[g][:],
                                                  in1=ps_r[:], op=ALU.mult))
                    outs.append(eng.tensor_tensor(out=xn[g][:], in0=tmpA[:],
                                                  in1=ps_m[:], op=ALU.subtract))
                    outs.append(eng.tensor_copy(out=xnb[g][:], in_=xn[g][:]))
                return outs
            dep_xn = [s_bc1, s_sq1]
            if c >= 1:
                dep_xn.append(prev[(c - 1, 'ffn_mm_last')])
                dep_xn.append(prev[(c - 1, 'out', 7)])
            s_xn = sch.step("vector", xn_fn, deps=dep_xn, small=True)
            prev[(c, 'xn')] = s_xn

            for k in range(NH):
                def gu_fn(eng, c=c, k=k):
                    outs = []
                    for j in range(8):
                        outs.append(eng.matmul(ps_g[:], wg[j][:, 128 * k:128 * (k + 1)],
                                               xnb[j][:], start=(j == 0), stop=(j == 7)))
                    for j in range(8):
                        outs.append(eng.matmul(ps_u[:], wu[j][:, 128 * k:128 * (k + 1)],
                                               xnb[j][:], start=(j == 0), stop=(j == 7)))
                    return outs
                dep_g = [s_xn] + (cvt[:32] if c == 0 and k == 0 else [])
                if (c, 'h', k - 1) in prev:
                    dep_g.append(prev[(c, 'h', k - 1)])
                elif c >= 1 and k == 0:
                    dep_g.append(prev[(c - 1, 'h', NH - 1)])
                s_gu = sch.step("tensor", gu_fn, deps=dep_g)

                def sil_fn(eng, c=c, k=k):
                    return [eng.activation(out=gsil[:], in_=ps_g[:], func=AF.Silu)]
                s_sil = sch.step("scalar", sil_fn, deps=[s_gu], small=True)

                def h_fn(eng, c=c, k=k):
                    return [eng.tensor_tensor(out=hply[k][:], in0=gsil[:],
                                              in1=ps_u[:], op=ALU.mult)]
                dep_h = [s_sil]
                if (c - 1, 'down_done') in prev and c >= 1:
                    dep_h.append(prev[(c - 1, 'down_done')])
                s_h = sch.step("vector", h_fn, deps=dep_h, small=True)
                prev[(c, 'h', k)] = s_h

            prev[(c, 'ffn_mm_last')] = prev[(c, 'h', NH - 1)]

            h_all = [prev[(c, 'h', k)] for k in range(NH)]
            down_mm_steps = []
            for m in range(8):
                def dn_fn(eng, c=c, m=m):
                    outs = []
                    for k in range(NH):
                        outs.append(eng.matmul(ps_d[m % 2][:],
                                               wd[k][:, 128 * m:128 * (m + 1)],
                                               hply[k][:], start=(k == 0),
                                               stop=(k == NH - 1)))
                    return outs
                dep_d = h_all + (cvt[32:] if c == 0 and m == 0 else [])
                if (c, 'fin', m - 2) in prev:
                    dep_d.append(prev[(c, 'fin', m - 2)])
                elif c >= 1 and m < 2:
                    dep_d.append(prev[(c - 1, 'fin', m + 6)])
                s_dn = sch.step("tensor", dn_fn, deps=dep_d)
                down_mm_steps.append(s_dn)

                @dmas(8)
                def ypl2_fn(eng, c=c, m=m):
                    return [eng.dma_start(
                        out=ypt[r][:],
                        in_=YP[r * DM + 128 * m:r * DM + 128 * (m + 1),
                               c * T:(c + 1) * T]) for r in range(8)]
                dep_yl = [prev[(c, 'fin', m - 1)]] if (c, 'fin', m - 1) in prev else \
                         ([prev[(c - 1, 'fin', 7)]] if c >= 1 else [])
                s_yl = sch.dma("gpsimd", ypl2_fn, deps=dep_yl)

                def fin_fn(eng, c=c, m=m):
                    outs = []
                    outs.append(eng.tensor_tensor(out=acc[:], in0=xc[m][:],
                                                  in1=xn[m][:], op=ALU.add))
                    outs.append(eng.tensor_tensor(out=acc[:], in0=acc[:],
                                                  in1=ps_d[m % 2][:], op=ALU.add))
                    outs.append(eng.tensor_scalar(out=acc[:], in0=acc[:],
                                                  scalar1=bd[:, m:m + 1], scalar2=None,
                                                  op0=ALU.add))
                    for r in range(7):
                        outs.append(eng.tensor_tensor(out=acc[:], in0=acc[:],
                                                      in1=ypt[r][:], op=ALU.add))
                    outs.append(eng.tensor_tensor(out=outt[m % 2][:], in0=acc[:],
                                                  in1=ypt[7][:], op=ALU.add))
                    return outs
                dep_f = [s_dn, s_yl, s_xn]
                if (c, 'store', m - 2) in prev:
                    dep_f.append(prev[(c, 'store', m - 2)])
                elif c >= 1 and m < 2:
                    dep_f.append(prev[(c - 1, 'store', m + 6)])
                s_fin = sch.step("vector", fin_fn, deps=dep_f, small=True)
                prev[(c, 'fin', m)] = s_fin

                @dmas(1)
                def ost_fn(eng, c=c, m=m):
                    return [eng.dma_start(
                        out=O_OUT[128 * m:128 * (m + 1), c * T:(c + 1) * T],
                        in_=outt[m % 2][:])]
                s_ost = sch.dma("gpsimd", ost_fn, deps=[s_fin])
                prev[(c, 'store', m)] = s_ost
                prev[(c, 'out', m)] = s_ost
                store_steps.append(s_ost)
            prev[(c, 'down_done')] = down_mm_steps[-1]

        def fin2_fn(eng):
            return [eng.memset(rowa[0:1, 0:1], 0.0)]
        sch.step("vector", fin2_fn, deps=store_steps)

        sch.emit(block)
    return nc


_cache = {}


def _get(name, builder):
    if name not in _cache:
        _cache[name] = builder()
    return _cache[name]


def kernel(L0_attn_X, L0_attn_ln_g, L0_attn_ln_b, L0_attn_W_up1, L0_attn_conv_w,
           L0_attn_conv_b, L0_attn_W_ll, L0_attn_b_ll, L0_attn_A_log,
           L0_attn_W_up2, L0_attn_W_down, L0_attn_b_down, L0_ffn_w_gate,
           L0_ffn_w_up, L0_ffn_w_down):
    f32 = np.float32
    xT = np.ascontiguousarray(np.asarray(L0_attn_X, f32)[0].T)
    wu1T = np.ascontiguousarray(np.asarray(L0_attn_W_up1, f32).T)
    wllT = np.ascontiguousarray(np.asarray(L0_attn_W_ll, f32).T)
    wu2T = np.ascontiguousarray(np.asarray(L0_attn_W_up2, f32).T)
    wdnT = np.ascontiguousarray(np.asarray(L0_attn_W_down, f32).T)
    cwr = np.ascontiguousarray(
        np.asarray(L0_attn_conv_w, f32)[:, 0, :].reshape(16, 128, 4)
        .transpose(1, 0, 2).reshape(128, 64))
    cbr = np.ascontiguousarray(np.asarray(L0_attn_conv_b, f32).reshape(16, 128).T)
    lngr = np.ascontiguousarray(np.asarray(L0_attn_ln_g, f32).reshape(8, 128).T)
    lnbr = np.ascontiguousarray(np.asarray(L0_attn_ln_b, f32).reshape(8, 128).T)
    bll_pad = np.zeros((17 * 128,), f32)
    bll_pad[:EOUT] = np.asarray(L0_attn_b_ll, f32)
    bllr = np.ascontiguousarray(bll_pad.reshape(17, 128).T)
    bdr = np.ascontiguousarray(np.asarray(L0_attn_b_down, f32).reshape(8, 128).T)
    wgT = np.ascontiguousarray(np.asarray(L0_ffn_w_gate, np.int32).T)
    wuT = np.ascontiguousarray(np.asarray(L0_ffn_w_up, np.int32).T)
    wdT = np.ascontiguousarray(np.asarray(L0_ffn_w_down, np.int32).T)
    alog = np.ascontiguousarray(np.asarray(L0_attn_A_log, f32))

    nc1 = _get("L1", build_L1)
    in1 = []
    for i in range(NCORE):
        lo = i * TC
        xsl = np.zeros((DM, TC + 3), f32)
        xsl[:, 3:] = xT[:, lo:lo + TC]
        if i > 0:
            xsl[:, 0:3] = xT[:, lo - 3:lo]
        hmv = np.zeros((1, 3), f32) if i == 0 else np.ones((1, 3), f32)
        in1.append(dict(X=xsl, HM=hmv, LNG=lngr, LNB=lnbr, WU1T=wu1T,
                        WLLT=wllT, CW=cwr, CB=cbr, BLL=bllr))
    r1 = run_bass_kernel_spmd(nc1, in1, core_ids=list(range(NCORE))).results

    delta_full = np.concatenate([np.asarray(r["O_DELTA"]) for r in r1], axis=1)
    xms_full = np.concatenate([np.asarray(r["O_XMS"]).view(BFNP)
                               if np.asarray(r["O_XMS"]).dtype != BFNP
                               else np.asarray(r["O_XMS"]) for r in r1], axis=1)
    bc_full = np.concatenate([np.asarray(r["O_BC"]).view(BFNP)
                              if np.asarray(r["O_BC"]).dtype != BFNP
                              else np.asarray(r["O_BC"]) for r in r1], axis=1)
    xn_full = np.concatenate([np.asarray(r["O_XN"]).view(BFNP)
                              if np.asarray(r["O_XN"]).dtype != BFNP
                              else np.asarray(r["O_XN"]) for r in r1], axis=1)

    nc2 = _get("L2", build_L2)
    in2 = []
    for j in range(NCORE):
        sl = slice(j * DSL, (j + 1) * DSL)
        in2.append(dict(
            DELTA=np.ascontiguousarray(delta_full[sl]),
            XMS=np.ascontiguousarray(xms_full[sl]),
            BFI=np.ascontiguousarray(bc_full[0:NST]),
            CFI=np.ascontiguousarray(bc_full[NST:2 * NST]),
            XNF=xn_full,
            ALOG=np.ascontiguousarray(alog[sl]),
            WU2T=np.ascontiguousarray(wu2T[:, sl]),
            WDT=np.ascontiguousarray(wdnT[sl]),
        ))
    r2 = run_bass_kernel_spmd(nc2, in2, core_ids=list(range(NCORE))).results
    yp_all = [np.asarray(r["O_YP"]) for r in r2]

    nc3 = _get("L3", build_L3)
    in3 = []
    for i in range(NCORE):
        lo = i * TC
        ypstack = np.ascontiguousarray(
            np.concatenate([yp[:, lo:lo + TC] for yp in yp_all], axis=0))
        in3.append(dict(
            X=np.ascontiguousarray(xT[:, lo:lo + TC]),
            YP=ypstack, BD=bdr, WGT=wgT, WUT=wuT, WDT3=wdT,
        ))
    r3 = run_bass_kernel_spmd(nc3, in3, core_ids=list(range(NCORE))).results
    out = np.concatenate([np.asarray(r["O_OUT"]) for r in r3], axis=1)
    return np.ascontiguousarray(out.T).reshape(1, LTOT, DM).astype(np.float32)
